# version 07
# baseline (speedup 1.0000x reference)
import math
import contextlib
import numpy as np
import ml_dtypes
import concourse.bass as bass
import concourse.mybir as mybir
from concourse.alu_op_type import AluOpType as OP
from concourse.bass_utils import run_bass_kernel_spmd

F32 = mybir.dt.float32
BF16 = mybir.dt.bfloat16
U32 = mybir.dt.uint32
I16 = mybir.dt.int16
AF = mybir.ActivationFunctionType
AX = mybir.AxisListType.X

NCORES = 8
B, L, D = 16, 2048, 1024
TPC = 4096
NT = 32
H, KV, HD = 16, 4, 64
E = 8
CAPG = 8192
CAPT = 72
SPE = NT * CAPT
NSLOT = E * SPE
NSB = SPE // 128
HSB = [512, 512, 512, 512, 256]
DFF = 4096
C0 = 24.0
EPS = 1e-5
ACUT = 36.0


def _kcmin(h):
    import math as _m
    s = 2.0 ** (-(2.0 ** -(_m.log2(H) - 3)))
    sl = s * (s ** h)
    kc = 0
    while sl * (1920.0 - 128.0 * kc) > ACUT:
        kc += 1
    return kc
ENGS = ("sync", "tensor", "vector", "scalar", "gpsimd")
DRAIN_ENGS = ("vector", "scalar", "gpsimd")


def _slopes():
    start = 2.0 ** (-(2.0 ** -(math.log2(H) - 3)))
    return [start * (start ** i) for i in range(H)]


def _bf(v):
    return float(ml_dtypes.bfloat16(v))


def _host_tables():
    inv = 1.0 / (10000.0 ** (np.arange(0, HD, 2, dtype=np.float64) / HD))
    t = np.arange(L, dtype=np.float64)
    fr = inv[:, None] * t[None, :]
    cosT = np.concatenate([np.cos(fr), np.cos(fr)], 0)
    sinT = np.concatenate([np.sin(fr), np.sin(fr)], 0)
    sgn = np.concatenate([-np.ones((32, 1)), np.ones((32, 1))], 0)
    sinS = sinT * sgn
    cosq = np.concatenate([cosT / 8.0, cosT / 8.0], 0)
    sinq = np.concatenate([sinS / 8.0, sinS / 8.0], 0)
    return (cosq.astype(np.float32), sinq.astype(np.float32),
            cosT.astype(np.float32), sinS.astype(np.float32))


class Prog:
    def __init__(self):
        self.q = {k: [] for k in ENGS}
        self.cnt = {}
        self.sems = None
        self.pending = {k: [] for k in ENGS}

    def tok(self, sem):
        return (sem, self.cnt.get(sem, 0))

    def fence(self):
        toks = [(s, v) for s, v in self.cnt.items() if v > 0]
        for k in ENGS:
            self.pending[k] = list(toks)

    DMA_SEMS = ("dmx0", "dmx1", "dsn0", "dsn1", "dtr", "dcc", "dwa", "dtb",
                "dwf0", "dwf1", "dgx0", "dgx1", "dro", "dsc", "dyw0", "dyw1",
                "daug")

    def emit(self, eng, fn, waits=(), inc=None, amt=1, drain=None):
        va = None
        selfwait = None
        if inc is not None:
            if inc in self.DMA_SEMS:
                prev = self.cnt.get(inc, 0)
                if prev > 0:
                    selfwait = (inc, prev)
            self.cnt[inc] = self.cnt.get(inc, 0) + amt
            va = self.cnt[inc]
        allw = [w for w in self.pending[eng] if w is not None]
        self.pending[eng] = []
        allw += [w for w in waits if w is not None]
        if selfwait is not None:
            allw.append(selfwait)
        allw = tuple((s, v) for (s, v) in allw if v > 0)
        do_drain = (eng in DRAIN_ENGS) if drain is None else drain

        def item(h, self=self, allw=allw, inc=inc, amt=amt, fn=fn,
                 do_drain=do_drain):
            seen = {}
            for (s, v) in allw:
                if seen.get(s, -1) < v:
                    h.wait_ge(self.sems[s], v)
                    seen[s] = v
            r = fn(h)
            if inc is not None:
                assert r is not None
                r.then_inc(self.sems[inc], amt)
            if do_drain:
                h.drain()

        self.q[eng].append(item)
        return (inc, va) if inc is not None else None

    def flush(self, block):
        for eng in ENGS:
            items, self.q[eng] = self.q[eng], []
            if not items:
                continue

            def _run(h, items=items):
                for it in items:
                    it(h)
            getattr(block, eng)(_run)


def build_nc():
    nc = bass.Bass("TRN2", target_bir_lowering=False, debug=False)

    x_in = nc.dram_tensor("x", [TPC, D], F32, kind="ExternalInput").ap()
    wq_in = nc.dram_tensor("wq", [D, D], F32, kind="ExternalInput").ap()
    wk_in = nc.dram_tensor("wk", [D, 256], F32, kind="ExternalInput").ap()
    wv_in = nc.dram_tensor("wv", [D, 256], F32, kind="ExternalInput").ap()
    wo_in = nc.dram_tensor("wo", [D, D], F32, kind="ExternalInput").ap()
    wr_in = nc.dram_tensor("wr", [D, E], F32, kind="ExternalInput").ap()
    br_in = nc.dram_tensor("br", [1, E], F32, kind="ExternalInput").ap()
    w1_in = nc.dram_tensor("w1", [E, D, DFF], F32, kind="ExternalInput").ap()
    b1_in = nc.dram_tensor("b1", [E, DFF], F32, kind="ExternalInput").ap()
    w2_in = nc.dram_tensor("w2", [E, DFF, D], F32, kind="ExternalInput").ap()
    b2_in = nc.dram_tensor("b2", [E, D], F32, kind="ExternalInput").ap()
    cosq_in = nc.dram_tensor("cosq", [128, L], F32, kind="ExternalInput").ap()
    sinq_in = nc.dram_tensor("sinq", [128, L], F32, kind="ExternalInput").ap()
    pmask_in = nc.dram_tensor("prevmask", [E, 1], F32, kind="ExternalInput").ap()
    y_out = nc.dram_tensor("y", [TPC, D], F32, kind="ExternalOutput").ap()

    import os as _os
    _dbg = dict(kind="ExternalOutput") if _os.environ.get("KDBG") else {}
    xn_hbm = nc.dram_tensor("xn_hbm", [TPC + 1, D], BF16, **_dbg).ap()
    o_dram = nc.dram_tensor("o_dram", [NSLOT, D], BF16, **_dbg).ap()
    cc_in = nc.dram_tensor("cc_in", [1, E], F32).ap()
    cc_out = nc.dram_tensor("cc_out", [NCORES, E], F32, **_dbg).ap()
    augd = nc.dram_tensor("augd", [7, 16], F32).ap()

    slopes = _slopes()
    P = Prog()
    T = P.tok
    EM = P.emit

    sem_names = ["dmx0", "dmx1", "dsn0", "dsn1", "dtr", "dcc", "dwa", "dtb",
                 "dwf0", "dwf1", "dgx0", "dgx1", "dro", "dsc", "dyw0", "dyw1",
                 "daug", "st", "sv", "ss", "sg"]

    with contextlib.ExitStack() as stack:
        sb = lambda *a: stack.enter_context(nc.sbuf_tensor(*a))
        sems = {n: stack.enter_context(nc.semaphore(n)) for n in sem_names}
        P.sems = sems

        ones_bf = sb("ones_bf", [128, 128], BF16)
        constp = sb("constp", [128, 8], F32)
        identf = sb("identf", [128, 128], F32)
        iota_p = sb("iota_p", [128, 1], F32)
        iota_j = sb("iota_j", [128, 128], F32)
        crossp = sb("crossp", [1, E], F32)
        ball_sb = sb("ball", [E, E], F32)
        pmask_sb = sb("pmask_sb", [E, 1], F32)
        wr_sb = sb("wr_sb", [128, 8 * E], BF16)
        br_sb = sb("br_sb", [1, E], BF16)
        jall = sb("jall", [128, NT * E], F32)
        gall = sb("gall", [128, NT * E], F32)
        b2all = sb("b2all", [E, D], BF16)

        block = stack.enter_context(nc.Block())

        for (dst, pat, cm) in (
            (iota_p[:], [[0, 1]], 1),
            (iota_j[:], [[1, 128]], 0),
        ):
            EM("gpsimd", lambda g, dst=dst, pat=pat, cm=cm: g.iota(
                dst, pattern=pat, base=0, channel_multiplier=cm,
                allow_small_or_imprecise_dtypes=True), inc="sg")
        tok_iota = T("sg")
        for (dst, src) in ((wr_sb[:].rearrange("p (c e) -> p c e", e=E),
                            wr_in[:].rearrange("(c p) e -> p c e", p=128)),
                           (br_sb[:], br_in[:]), (pmask_sb[:], pmask_in[:])):
            EM("gpsimd", lambda g, dst=dst, src=src: g.dma_start(out=dst, in_=src),
               inc="dwa", amt=16)
        tok_smallw = T("dwa")

        EM("vector", lambda v: v.memset(constp[:, 0:1], EPS))
        nc.const_aps.register(F32, EPS, constp[:, 0:1])
        EM("vector", lambda v: v.memset(constp[:, 1:2], -0.5))
        nc.const_aps.register(F32, -0.5, constp[:, 1:2])
        EM("vector", lambda v: v.memset(constp[:, 2:3], 0.0))
        EM("vector", lambda v: v.memset(ones_bf[:], 1.0))
        EM("vector", lambda v: v.memset(zrow[:], 0.0))
        EM("vector", lambda v: v.tensor_scalar(
            identf[:], iota_j[:], iota_p[:], None, op0=OP.is_equal),
            waits=[tok_iota], inc="sv")
        tok_consts = T("sv")

        with contextlib.ExitStack() as st1:
            sb1 = lambda *a: st1.enter_context(nc.sbuf_tensor(*a))
            zrow = sb1("zrow", [1, D], BF16)
            xbuf = sb1("xbuf", [128, 2 * D], F32)
            xcbuf = sb1("xcbuf", [128, 2 * D], F32)
            xnbuf = sb1("xnbuf", [128, 2 * D], BF16)
            stats = sb1("stats", [128, 16], F32)
            EM("sync", lambda s: s.dma_start(out=xn_hbm[TPC:TPC + 1, :],
                                             in_=zrow[:]),
               waits=[tok_consts], inc="dtr", amt=16)

            prev_xn = [None, None]
            for i in range(NT):
                par = i % 2
                so = 8 * par
                ld = EM("sync", lambda s, i=i, par=par: s.dma_start(
                    out=xbuf[:, par * D:(par + 1) * D],
                    in_=x_in[i * 128:(i + 1) * 128, :]),
                    waits=[prev_xn[par]], inc="dmx%d" % par, amt=16)
                xt = xbuf[:, par * D:(par + 1) * D]
                xct = xcbuf[:, par * D:(par + 1) * D]
                xnt = xnbuf[:, par * D:(par + 1) * D]
                EM("vector", lambda v, xt=xt, so=so: v.reduce_sum(
                    stats[:, so:so + 1], xt, axis=AX), waits=[ld])
                EM("vector", lambda v, so=so: v.tensor_scalar(
                    stats[:, so + 2:so + 3], stats[:, so:so + 1], -1.0 / D, None,
                    op0=OP.mult))
                EM("vector", lambda v, so=so: v.tensor_tensor(
                    stats[:, so + 3:so + 4], stats[:, so + 2:so + 3],
                    stats[:, so + 2:so + 3], op=OP.mult))
                EM("vector", lambda v, xt=xt, xct=xct: v.tensor_tensor(
                    xct, xt, xt, op=OP.mult))
                EM("vector", lambda v, xct=xct, so=so: v.reduce_sum(
                    stats[:, so + 1:so + 2], xct, axis=AX))
                stt = EM("vector", lambda v, so=so: v.scalar_tensor_tensor(
                    stats[:, so + 4:so + 5], stats[:, so + 1:so + 2], 1.0 / D,
                    stats[:, so + 3:so + 4], op0=OP.mult, op1=OP.subtract),
                    inc="sv")
                EM("scalar", lambda sc, so=so: sc.activation(
                    stats[:, so + 5:so + 6], stats[:, so + 4:so + 5], AF.Ln,
                    bias=EPS), waits=[stt])
                rstd_tok = EM("scalar", lambda sc, so=so: sc.activation(
                    stats[:, so + 7:so + 8], stats[:, so + 5:so + 6], AF.Exp,
                    scale=-0.5), inc="ss")
                xn_done = EM("vector", lambda v, xnt=xnt, xt=xt, so=so:
                             v.tensor_scalar(
                                 xnt, xt, stats[:, so + 2:so + 3],
                                 stats[:, so + 7:so + 8],
                                 op0=OP.add, op1=OP.mult),
                             waits=[rstd_tok], inc="sv")
                prev_xn[par] = EM("sync", lambda s, i=i, xnt=xnt: s.dma_start(
                    out=xn_hbm[i * 128:(i + 1) * 128, :], in_=xnt),
                    waits=[xn_done], inc="dsn%d" % par, amt=16)
            tok_xn_all = [T("dsn0"), T("dsn1")]
            P.flush(block)

        P.fence()
        with contextlib.ExitStack() as stA:
            sbA = lambda *a: stA.enter_context(nc.sbuf_tensor(*a))
            xnT = sbA("xnT", [128, 8 * TPC], BF16)
            wqo_sb = sbA("wqo_sb", [128, 8 * D], BF16)
            wkv_sb = sbA("wkv_sb", [128, 8 * 512], BF16)
            cosq_sb = sbA("cosq_sb", [128, L], BF16)
            sinq_sb = sbA("sinq_sb", [128, L], BF16)
            qaug0 = sbA("qaug0", [71, L], BF16)
            qaug1 = sbA("qaug1", [71, L], BF16)
            qaug2 = sbA("qaug2", [71, L], BF16)
            qaug3 = sbA("qaug3", [71, L], BF16)
            kaug0 = sbA("kaug0", [71, L], BF16)
            kaug1 = sbA("kaug1", [71, L], BF16)
            kaug2 = sbA("kaug2", [71, L], BF16)
            kaug3 = sbA("kaug3", [71, L], BF16)
            augcolT = sbA("augcolT", [71, 16], F32)
            vaug = sbA("vaug", [128, 16 * 128], BF16)
            attno = sbA("attno", [128, 8 * L], BF16)
            ptil = sbA("ptil", [128, 2 * 1024], BF16)
            rtmp = sbA("rtmp", [128, L], F32)
            lgtop = sbA("lgtop", [128, 8], F32)
            gtmp = sbA("gtmp", [128, 4], F32)
            qaug = [qaug0, qaug1, qaug2, qaug3]
            kaug = [kaug0, kaug1, kaug2, kaug3]

            for cch in range(8):
                EM("sync", lambda s, cch=cch: s.dma_start_transpose(
                    xnT[:, cch * TPC:(cch + 1) * TPC],
                    xn_hbm[0:TPC, cch * 128:(cch + 1) * 128]),
                    waits=tok_xn_all if cch == 0 else [], inc="dtr", amt=16)
            tok_xnT = T("dtr")

            for cch in range(8):
                EM("gpsimd", lambda g, cch=cch: g.dma_start(
                    out=wqo_sb[:, cch * D:(cch + 1) * D],
                    in_=wq_in[cch * 128:(cch + 1) * 128, :]), inc="dwa", amt=16)
            for cch in range(8):
                EM("gpsimd", lambda g, cch=cch: g.dma_start(
                    out=wkv_sb[:, cch * 512:cch * 512 + 256],
                    in_=wk_in[cch * 128:(cch + 1) * 128, :]), inc="dwa", amt=16)
                EM("gpsimd", lambda g, cch=cch: g.dma_start(
                    out=wkv_sb[:, cch * 512 + 256:cch * 512 + 512],
                    in_=wv_in[cch * 128:(cch + 1) * 128, :]), inc="dwa", amt=16)
            for dst, src in ((cosq_sb[:], cosq_in[:]), (sinq_sb[:], sinq_in[:])):
                EM("gpsimd", lambda g, dst=dst, src=src: g.dma_start(
                    out=dst, in_=src), inc="dwa", amt=16)
            tok_attw = T("dwa")

            for kv in range(KV):
                EM("vector", lambda v, kv=kv: v.memset(kaug[kv][64:71, :], 0.0))
            for s_ in range(4):
                EM("vector", lambda v, s_=s_: v.memset(qaug[s_][64:71, :], 0.0))
            for kc in range(16):
                EM("vector", lambda v, kc=kc: v.memset(
                    vaug[:, kc * 128 + 64:kc * 128 + 128], 1.0))
            tok_augs = EM("vector", lambda v: v.memset(
                vaug[0:1, 0:1], 1.0), inc="sv")
            tok_augs = T("sv")

            with contextlib.ExitStack() as st2:
                sb2 = lambda *a: st2.enter_context(nc.sbuf_tensor(*a))
                U0 = sb2("U0", [128, 128], BF16)
                selbf = sb2("selbf", [128, NT * E], BF16)
                selff = sb2("selff", [128, NT * E], F32)
                g1c = sb2("g1c", [128, NT], F32)
                brow = sb2("brow", [1, 4 * NT * E], F32)
                bgl_b = sb2("bgl_b", [128, NT * E], F32)
                sltmp = st2.enter_context(nc.sbuf_tensor("sltmp", [128, 2 * NT * E], F32))
                ps_lg = st2.enter_context(nc.psum_tensor("ps_lg", [128, 512], F32))
                ps_rw = st2.enter_context(nc.psum_tensor("ps_rw", [128, 512], F32))
                ps_ro = st2.enter_context(nc.psum_tensor("ps_ro", [128, 512], F32))
                prev_stage = None
                for which, rows in (("hi", (64, 66, 68)), ("lo", (65, 67, 69)),
                                    ("one", (70,))):
                    if which == "one":
                        stg = EM("vector", lambda v: v.memset(
                            ptil[0:1, 0:L], 1.0), waits=[prev_stage], inc="sv")
                    else:
                        pat = ([[1, 128], [0, 16]] if which == "hi"
                               else [[0, 128], [1, 16]])
                        stg = EM("gpsimd", lambda g, pat=pat: g.iota(
                            ptil[0:1, 0:L].rearrange("a (h l) -> a h l", l=16),
                            pattern=pat, base=0, channel_multiplier=0,
                            allow_small_or_imprecise_dtypes=True),
                            waits=[prev_stage], inc="sg")
                    last = None
                    for kv in range(KV):
                        for r in rows:
                            last = EM("sync", lambda s, kv=kv, r=r: s.dma_start(
                                out=kaug[kv][r:r + 1, :], in_=ptil[0:1, 0:L]),
                                waits=[stg, tok_augs], inc="dtr", amt=16)
                    prev_stage = last
                tok_krows = prev_stage
                vrow = None
                for h in range(H):
                    s0v = slopes[h]
                    hi_ = _bf(s0v)
                    mid_ = _bf(s0v - hi_)
                    lo_ = _bf(s0v - hi_ - mid_)
                    r70_ = _bf(-(C0 + 2047.0 * s0v))
                    for r, vv in enumerate([16.0 * hi_, hi_, 16.0 * mid_, mid_,
                                            16.0 * lo_, lo_, r70_]):
                        vrow = EM("vector", lambda v, r=r, h=h, vv=vv: v.memset(
                            rtmp[0:1, r * 16 + h:r * 16 + h + 1], vv))
                vrow = EM("vector", lambda v: v.tensor_copy(
                    rtmp[0:1, 112:113], rtmp[0:1, 0:1]), inc="sv")
                aw = EM("sync", lambda s: s.dma_start(
                    out=augd[:].rearrange("r h -> (r h)")[None, :],
                    in_=rtmp[0:1, 0:112]), waits=[vrow], inc="dtr", amt=16)
                ar = EM("sync", lambda s: s.dma_start(
                    out=augcolT[64:71, :], in_=augd[:]), waits=[aw],
                    inc="dtr", amt=16)
                EM("vector", lambda v: v.tensor_scalar(
                    U0[:], iota_j[:], iota_p[:], None, op0=OP.is_gt),
                    waits=[tok_iota], inc="sv")
                tok_u0 = T("sv")
                rt_tok = [None, None]
                for i in range(NT):
                    par = i % 2
                    pslice = ps_lg[:, par * 8:par * 8 + 8]

                    def lgmm(t, i=i, pslice=pslice):
                        for cch in range(8):
                            t.matmul(pslice,
                                     xnT[:, cch * TPC + i * 128:
                                         cch * TPC + (i + 1) * 128],
                                     wr_sb[:, cch * E:(cch + 1) * E],
                                     start=(cch == 0), stop=False)
                        return t.matmul(pslice, ones_bf[0:1, :], br_sb[:],
                                        start=False, stop=True)
                    mm = EM("tensor", lgmm,
                            waits=[tok_xnT, tok_smallw, rt_tok[par]], inc="st")
                    EM("vector", lambda v, pslice=pslice: v.max(
                        out=lgtop[:], in_=pslice), waits=[mm])
                    nv1 = EM("vector", lambda v: v.tensor_scalar(
                        gtmp[:, 0:1], lgtop[:, 0:1], -1.0, None, op0=OP.mult),
                        inc="sv")
                    ex = EM("scalar", lambda sc: sc.activation(
                        gtmp[:, 1:2], lgtop[:, 1:2], AF.Exp, bias=gtmp[:, 0:1]),
                        waits=[nv1], inc="ss")
                    exg = EM("scalar", lambda sc, i=i, pslice=pslice:
                             sc.activation(
                                 gall[:, i * E:(i + 1) * E], pslice, AF.Exp,
                                 bias=gtmp[:, 0:1]), inc="ss")
                    EM("vector", lambda v, i=i, pslice=pslice: v.tensor_scalar(
                        selff[:, i * E:(i + 1) * E], pslice, lgtop[:, 1:2], None,
                        op0=OP.is_ge))
                    EM("vector", lambda v, i=i: v.tensor_copy(
                        selbf[:, i * E:(i + 1) * E], selff[:, i * E:(i + 1) * E]))
                    EM("vector", lambda v: v.tensor_scalar(
                        gtmp[:, 2:3], gtmp[:, 1:2], 1.0, None, op0=OP.add),
                        waits=[ex])
                    EM("vector", lambda v, i=i: v.reciprocal(
                        g1c[:, i:i + 1], gtmp[:, 2:3]))
                    rt_tok[par] = EM("vector", lambda v, i=i: v.tensor_scalar(
                        gall[:, i * E:(i + 1) * E], gall[:, i * E:(i + 1) * E],
                        g1c[:, i:i + 1], None, op0=OP.mult),
                        waits=[exg], inc="sv")
                tok_rout = T("sv")

                cmm = EM("tensor", lambda t: t.matmul(
                    ps_ro[0:1, 0:NT * E], ones_bf[:, 0:1], selbf[:],
                    start=True, stop=True), waits=[tok_rout], inc="st")
                rwm = EM("tensor", lambda t: t.matmul(
                    ps_rw[:, 0:NT * E], U0[:], selbf[:], start=True, stop=True),
                    waits=[tok_u0], inc="st")
                EM("vector", lambda v: v.tensor_copy(
                    brow[:, 0:NT * E], ps_ro[0:1, 0:NT * E]), waits=[cmm])
                for e in range(E):
                    EM("vector", lambda v, e=e: v.tensor_tensor_scan(
                        brow[:, NT * E + e:2 * NT * E:E], brow[:, e:NT * E:E],
                        brow[:, e:NT * E:E], 0.0, op0=OP.add, op1=OP.bypass))
                bv = EM("vector", lambda v: v.tensor_tensor(
                    brow[:, 2 * NT * E:3 * NT * E], brow[:, NT * E:2 * NT * E],
                    brow[:, 0:NT * E], op=OP.subtract), inc="sv")
                ccw = EM("sync", lambda s: s.dma_start(
                    out=cc_in[:],
                    in_=brow[:, NT * E + (NT - 1) * E:2 * NT * E]),
                    waits=[bv], inc="dcc", amt=16)
                cc = EM("gpsimd", lambda g: g.collective_compute(
                    "AllGather", OP.bypass, replica_groups=[list(range(NCORES))],
                    ins=[cc_in[:].opt()], outs=[cc_out[:].opt()]),
                    waits=[ccw], inc="sg")
                ccr = EM("sync", lambda s: s.dma_start(
                    out=ball_sb[:], in_=cc_out[:]), waits=[cc], inc="dcc", amt=16)
                cpm = EM("tensor", lambda t: t.matmul(
                    ps_ro[0:1, 64:64 + E], pmask_sb[:], ball_sb[:],
                    start=True, stop=True), waits=[ccr, tok_smallw], inc="st")
                EM("vector", lambda v: v.tensor_copy(
                    crossp[:], ps_ro[0:1, 64:64 + E]), waits=[cpm])
                for i in range(NT):
                    last_add = EM("vector", lambda v, i=i: v.tensor_tensor(
                        brow[:, 3 * NT * E + i * E:3 * NT * E + (i + 1) * E],
                        brow[:, 2 * NT * E + i * E:2 * NT * E + (i + 1) * E],
                        crossp[:], op=OP.add),
                        inc="sv" if i == NT - 1 else None)
                bases = last_add
                ones_f = sb2("ones_f", [1, 128], F32)
                of = EM("vector", lambda v: v.memset(ones_f[:], 1.0), inc="sv")
                bcm = EM("tensor", lambda t: t.matmul(
                    ps_ro[:, 256:256 + NT * E], ones_f[:],
                    brow[0:1, 3 * NT * E:4 * NT * E], start=True, stop=True),
                    waits=[bases, of], inc="st")
                bglev = EM("vector", lambda v: v.tensor_copy(
                    bgl_b[:], ps_ro[:, 256:256 + NT * E]), waits=[bcm],
                    inc="sv")

                Am = sltmp[:, 0:NT * E]
                Jt = sltmp[:, NT * E:2 * NT * E]
                EM("vector", lambda v: v.tensor_tensor(
                    Am, ps_rw[:, 0:NT * E], bgl_b[:], op=OP.add),
                    waits=[rwm, bglev, tok_augs])
                EM("vector", lambda v: v.tensor_scalar(
                    Am, Am, float(CAPG), None, op0=OP.is_lt))
                EM("vector", lambda v: v.tensor_tensor(Am, Am, selff[:], op=OP.mult))
                EM("vector", lambda v: v.tensor_tensor(
                    gall[:], gall[:], Am, op=OP.mult))
                skb = EM("vector", lambda v: v.tensor_copy(selbf[:], Am),
                         inc="sv")
                rwm2 = EM("tensor", lambda t: t.matmul(
                    ps_lg[:, 0:NT * E], U0[:], selbf[:], start=True, stop=True),
                    waits=[skb], inc="st")
                EM("vector", lambda v: v.tensor_scalar(
                    Jt, Am, -float(CAPT), float(CAPT), op0=OP.mult, op1=OP.add))
                EM("vector", lambda v: v.tensor_tensor(
                    jall[:], ps_lg[:, 0:NT * E], Jt, op=OP.add),
                    waits=[rwm2], inc="sv")
                tok_route = T("sv")
                P.flush(block)

            P.fence()
            with contextlib.ExitStack() as st4:
                xrb = st4.enter_context(nc.sbuf_tensor("xrb", [128, 2 * D], F32))
                yob = st4.enter_context(nc.sbuf_tensor("yob", [128, 2 * D], F32))
                ps_proj = st4.enter_context(nc.psum_tensor("ps_proj", [128, 2048], F32))
                ps_pv = st4.enter_context(nc.psum_tensor("ps_pv", [128, 1024], F32))
                ps_wo = st4.enter_context(nc.psum_tensor("ps_wo", [128, 1024], F32))
                lproj = None
                pvread = None
                pwread = [None, None]
                yev = [None, None]
                wo_done = None
                for b in range(2):
                    tb = b * L
                    if b == 1:
                        for cch in range(8):
                            EM("gpsimd", lambda g, cch=cch: g.dma_start(
                                out=wqo_sb[:, cch * D:(cch + 1) * D],
                                in_=wq_in[cch * 128:(cch + 1) * 128, :]),
                                waits=[wo_done] if cch == 0 else [],
                                inc="dwa", amt=16)
                    tok_wq = T("dwa")
                    for kv in range(KV):
                        for s_ in range(4):
                            h = 4 * kv + s_
                            EM("vector", lambda v, s_=s_, h=h: v.tensor_scalar(
                                qaug[s_][64:71, :], qaug[s_][64:71, :], 0.0,
                                augcolT[64:71, h:h + 1], op0=OP.mult, op1=OP.add),
                                waits=[tok_augs, lproj, T("st"),
                                       ar if (b, kv) == (0, 0) else None],
                                inc="sv" if s_ == 3 else None)
                        tok_aug_inj = T("sv")

                        for dch in range(2):
                            col0 = 4 * kv * HD + dch * 128

                            def qproj(t, col0=col0, tb=tb):
                                r = None
                                for cch in range(8):
                                    for f in range(4):
                                        r = t.matmul(
                                            ps_proj[:, f * 512:(f + 1) * 512],
                                            wqo_sb[:, cch * D + col0:
                                                   cch * D + col0 + 128],
                                            xnT[:, cch * TPC + tb + f * 512:
                                                cch * TPC + tb + (f + 1) * 512],
                                            start=(cch == 0), stop=(cch == 7))
                                return r
                            qm = EM("tensor", qproj,
                                    waits=[tok_attw if b == 0 else tok_wq,
                                           tok_xnT, lproj], inc="st")
                            for hh in range(2):
                                s_ = 2 * dch + hh
                                base = hh * 64
                                EM("vector", lambda v, s_=s_, base=base:
                                   v.tensor_tensor(
                                       qaug[s_][0:64, :],
                                       ps_proj[base:base + 64, 0:2048],
                                       cosq_sb[0:64, :], op=OP.mult),
                                   waits=[qm] if hh == 0 else [])
                                EM("vector", lambda v, base=base:
                                   v.tensor_tensor(
                                       rtmp[0:32, :],
                                       ps_proj[base + 32:base + 64, 0:2048],
                                       sinq_sb[0:32, :], op=OP.mult))
                                EM("vector", lambda v, base=base:
                                   v.tensor_tensor(
                                       rtmp[32:64, :],
                                       ps_proj[base:base + 32, 0:2048],
                                       sinq_sb[32:64, :], op=OP.mult))
                                lproj = EM("vector", lambda v, s_=s_:
                                           v.tensor_tensor(
                                               qaug[s_][0:64, :],
                                               qaug[s_][0:64, :],
                                               rtmp[0:64, :], op=OP.add),
                                           inc="sv" if hh == 1 else None)

                        def kproj(t, kv=kv, tb=tb):
                            r = None
                            for cch in range(8):
                                for f in range(4):
                                    r = t.matmul(
                                        ps_proj[0:64, f * 512:(f + 1) * 512],
                                        wkv_sb[:, cch * 512 + kv * 64:
                                               cch * 512 + kv * 64 + 64],
                                        xnT[:, cch * TPC + tb + f * 512:
                                            cch * TPC + tb + (f + 1) * 512],
                                        start=(cch == 0), stop=(cch == 7))
                            return r
                        km = EM("tensor", kproj, waits=[tok_attw, lproj], inc="st")
                        EM("vector", lambda v, kv=kv: v.tensor_tensor(
                            kaug[kv][0:64, :], ps_proj[0:64, 0:2048],
                            cosq_sb[0:64, :], op=OP.mult), waits=[km])
                        EM("vector", lambda v: v.tensor_tensor(
                            rtmp[0:32, :], ps_proj[32:64, 0:2048],
                            sinq_sb[0:32, :], op=OP.mult))
                        EM("vector", lambda v: v.tensor_tensor(
                            rtmp[32:64, :], ps_proj[0:32, 0:2048],
                            sinq_sb[32:64, :], op=OP.mult))
                        EM("vector", lambda v, kv=kv: v.tensor_tensor(
                            kaug[kv][0:64, :], kaug[kv][0:64, :], rtmp[0:64, :],
                            op=OP.add))
                        lproj = EM("vector", lambda v, kv=kv: v.tensor_scalar(
                            kaug[kv][0:64, :], kaug[kv][0:64, :], 8.0, None,
                            op0=OP.mult), inc="sv")

                        vev_pp = [None, None]
                        vtc0 = min(_kcmin(4 * kv + s_) for s_ in range(4))
                        for tc in range(vtc0, 16):
                            pslot = (ps_pv[:, 0:64] if tc % 2 == 0
                                     else ps_pv[:, 512:576])
                            w = []
                            if tc == vtc0:
                                w.append(pvread)
                            if vev_pp[tc % 2] is not None:
                                w.append(vev_pp[tc % 2])

                            def vproj(t, tc=tc, kv=kv, tb=tb, pslot=pslot):
                                r = None
                                for cch in range(8):
                                    r = t.matmul(
                                        pslot,
                                        xnT[:, cch * TPC + tb + tc * 128:
                                            cch * TPC + tb + (tc + 1) * 128],
                                        wkv_sb[:, cch * 512 + 256 + kv * 64:
                                               cch * 512 + 256 + kv * 64 + 64],
                                        start=(cch == 0), stop=(cch == 7))
                                return r
                            vm = EM("tensor", vproj, waits=w, inc="st")
                            vev_pp[tc % 2] = EM(
                                "scalar", lambda sc, tc=tc, pslot=pslot:
                                sc.activation(vaug[:, tc * 128:tc * 128 + 64],
                                              pslot, AF.Copy),
                                waits=[vm], inc="ss", drain=False)
                        tok_vaug = vev_pp[1]

                        for s_ in range(4):
                            h = 4 * kv + s_
                            hc = h // 2
                            hrow = (h % 2) * 64
                            kc0 = _kcmin(h)
                            for qh in range(2):
                                q0 = qh * 1024
                                exp_toks = []
                                pv_last = None
                                for kc in range(kc0, 16):
                                    par = kc % 2
                                    sbh = ps_proj[:, par * 1024:(par + 1) * 1024]
                                    w = ([tok_aug_inj, lproj, tok_vaug]
                                         if kc == kc0 else [])

                                    def smm(t, kv=kv, s_=s_, kc=kc, q0=q0, sbh=sbh):
                                        r = None
                                        for f in range(2):
                                            r = t.matmul(
                                                sbh[:, f * 512:(f + 1) * 512],
                                                kaug[kv][:, kc * 128:(kc + 1) * 128],
                                                qaug[s_][:, q0 + f * 512:
                                                          q0 + (f + 1) * 512],
                                                start=True, stop=True)
                                        return r
                                    sm = EM("tensor", smm, waits=w, inc="st")
                                    ex = EM("scalar",
                                            lambda sc, par=par, sbh=sbh:
                                            sc.activation(
                                                ptil[:, par * 1024:
                                                     (par + 1) * 1024],
                                                sbh, AF.Exp),
                                            waits=[sm], inc="ss", drain=False)
                                    exp_toks.append(ex)
                                    wpv = [ex]
                                    if kc == kc0:
                                        wpv.append(pvread)

                                    def pvmm(t, kc=kc, par=par, kc0=kc0):
                                        r = None
                                        for f in range(2):
                                            r = t.matmul(
                                                ps_pv[:, f * 512:(f + 1) * 512],
                                                vaug[:, kc * 128:(kc + 1) * 128],
                                                ptil[:, par * 1024 + f * 512:
                                                     par * 1024 + (f + 1) * 512],
                                                start=(kc == kc0),
                                                stop=(kc == 15))
                                        return r
                                    pv_last = EM("tensor", pvmm, waits=wpv,
                                                 inc="st")
                                EM("scalar", lambda sc, hrow=hrow:
                                   sc.activation(
                                       rtmp[hrow:hrow + 64, 0:1024],
                                       ps_pv[64:128, 0:1024], AF.Ln,
                                       bias=constp[hrow:hrow + 64, 2:3]),
                                   waits=[pv_last])
                                EM("scalar", lambda sc, hrow=hrow:
                                   sc.activation(
                                       rtmp[hrow:hrow + 64, 0:1024],
                                       rtmp[hrow:hrow + 64, 0:1024], AF.Exp,
                                       scale=-1.0), inc="ss", drain=False)
                                rdone = T("ss")
                                pvread = EM(
                                    "vector",
                                    lambda v, hc=hc, hrow=hrow, q0=q0:
                                    v.tensor_tensor(
                                        attno[hrow:hrow + 64,
                                              hc * L + q0:hc * L + q0 + 1024],
                                        ps_pv[0:64, 0:1024],
                                        rtmp[hrow:hrow + 64, 0:1024],
                                        op=OP.mult), waits=[rdone], inc="sv",
                                    drain=False)
                                lproj = exp_toks[-1]

                    for cch in range(8):
                        EM("gpsimd", lambda g, cch=cch: g.dma_start(
                            out=wqo_sb[:, cch * D:(cch + 1) * D],
                            in_=wo_in[cch * 128:(cch + 1) * 128, :]),
                            waits=[T("st")] if cch == 0 else [],
                            inc="dwa", amt=16)
                    tok_wo = T("dwa")

                    for tc in range(16):
                        par = tc % 2
                        pw = ps_wo if par == 0 else ps_pv
                        w = [tok_wo] if tc == 0 else []
                        if tc == 1:
                            w.append(pvread)
                        if pwread[par] is not None:
                            w.append(pwread[par])

                        def womm(t, tc=tc, pw=pw):
                            r = None
                            for hcc in range(8):
                                for f in range(2):
                                    r = t.matmul(
                                        pw[:, f * 512:(f + 1) * 512],
                                        attno[:, hcc * L + tc * 128:
                                              hcc * L + (tc + 1) * 128],
                                        wqo_sb[:, hcc * D + f * 512:
                                               hcc * D + (f + 1) * 512],
                                        start=(hcc == 0), stop=(hcc == 7))
                            return r
                        wm = EM("tensor", womm, waits=w, inc="st")
                        xl = EM("sync", lambda s, b=b, tc=tc, par=par: s.dma_start(
                            out=xrb[:, par * D:(par + 1) * D],
                            in_=x_in[b * L + tc * 128:b * L + (tc + 1) * 128, :]),
                            waits=[pwread[par]], inc="dmx%d" % par, amt=16)
                        ye = EM("vector", lambda v, par=par, pw=pw:
                                v.tensor_tensor(
                                    yob[:, par * D:(par + 1) * D], pw[:, 0:1024],
                                    xrb[:, par * D:(par + 1) * D], op=OP.add),
                                waits=[wm, xl, yev[par]], inc="sv", drain=False)
                        pwread[par] = ye
                        yev[par] = EM("sync", lambda s, b=b, tc=tc, par=par:
                                      s.dma_start(
                                          out=y_out[b * L + tc * 128:
                                                    b * L + (tc + 1) * 128, :],
                                          in_=yob[:, par * D:(par + 1) * D]),
                                      waits=[ye], inc="dyw%d" % par, amt=16)
                    wo_done = T("st")
                    pvread = pwread[1]
                tok_y_all = [T("dyw0"), T("dyw1")]
                P.flush(block)

        P.fence()
        with contextlib.ExitStack() as stF:
            sbF = lambda *a: stF.enter_context(nc.sbuf_tensor(*a))
            wf1a = sbF("wf1a", [128, 8 * 1024], BF16)
            wf1b = sbF("wf1b", [128, 8 * 1024], BF16)
            wf2s = sbF("wf2s", [128, 8 * 1024], BF16)
            xgt = sbF("xgt", [128, 8 * SPE], BF16)
            hsb = sbF("hsb", [128, 8 * SPE], BF16)
            oacc = sbF("oacc", [128, NSB * D], F32)
            xnb = sbF("xnb", [128, 4 * D], BF16)
            ohb = sbF("ohb", [128, 2 * 2 * CAPT], BF16)
            b1r = sbF("b1r", [32, 128], F32)
            b1c = sbF("b1c", [128, E * 32], F32)
            ps_f = stF.enter_context(nc.psum_tensor("ps_f", [128, 4096], F32))
            wf1 = [wf1a, wf1b]
            HOFF = [0, 512, 1024, 1536, 2048]
            OA, OB = 2560, 3072
            DBASE = [3584, 2560]

            EM("gpsimd", lambda g: g.dma_start(out=b2all[:], in_=b2_in[:]),
               inc="dwa", amt=16)
            tok_b2 = T("dwa")
            prev_b1ev = None
            b1tok = {}
            for e in range(E):
                bpar = e % 2
                l1 = EM("gpsimd", lambda g, e=e: g.dma_start(
                    out=b1r[:],
                    in_=b1_in[e].rearrange("(a b) -> a b", a=32)),
                    waits=[b1tok.get(e - 1)], inc="dwa", amt=16)
                tt = EM("tensor", lambda t: t.transpose(
                    ps_f[0:128, 3584:3584 + 32],
                    b1r[:],
                    identf[0:32, 0:32]),
                    waits=[l1, tok_consts, prev_b1ev], inc="st")
                b1tok[e] = EM("scalar", lambda sc, e=e: sc.copy(
                    b1c[:, e * 32:(e + 1) * 32],
                    ps_f[0:128, 3584:3584 + 32]), waits=[tt], inc="ss")
                prev_b1ev = b1tok[e]

            def w1load(g, e, q, par):
                return g.dma_start(
                    out=wf1[par][:].rearrange("p (c n) -> p c n", c=8),
                    in_=w1_in[e].rearrange("(c p) n -> p c n", p=128)[
                        :, :, q * 1024:(q + 1) * 1024])

            def w2load(g, e, q):
                return g.dma_start(
                    out=wf2s[:].rearrange("p (c n) -> p c n", c=8),
                    in_=w2_in[e, q * 1024:(q + 1) * 1024, :]
                    .rearrange("(c p) n -> p c n", p=128))

            wtok = {}
            for k in range(2):
                EM("gpsimd", lambda g, k=k: w1load(g, k // 4, k % 4, k % 2),
                   inc="dwf%d" % (k % 2), amt=16)
                wtok[k] = T("dwf%d" % (k % 2))
            EM("gpsimd", lambda g: w2load(g, 0, 0), inc="dtb", amt=16)
            w2tok = {0: T("dtb")}

            ohtok = [None, None]
            ohread = [None, None]
            xnread = [None, None]
            dev_tok = [None, None]
            xgt_read = None
            oacc_flush = None
            o_flush_tok = {}
            hread = [None] * len(HSB)
            oevh = [None, None]
            w2read = None
            for e in range(E):
                for g2 in range(16):
                    par = g2 % 2
                    xl = EM("sync", lambda s, g2=g2, par=par: s.dma_start(
                        out=xnb[:, par * 2048:(par + 1) * 2048]
                        .rearrange("p (t d) -> p t d", d=D),
                        in_=xn_hbm[g2 * 256:(g2 + 1) * 256, :]
                        .rearrange("(t p) d -> p t d", p=128)),
                        waits=[xnread[par]] + (tok_xn_all
                                               if (e, g2) == (0, 0) else []),
                        inc="dmx%d" % par, amt=16)
                    oht = None
                    for t2 in range(2):
                        oht = EM("vector", lambda v, g2=g2, t2=t2, par=par,
                                 e=e: v.tensor_scalar(
                                     ohb[:, par * 2 * CAPT + t2 * CAPT:
                                         par * 2 * CAPT + (t2 + 1) * CAPT],
                                     iota_j[:, 0:CAPT],
                                     jall[:, (g2 * 2 + t2) * E + e:
                                          (g2 * 2 + t2) * E + e + 1],
                                     None, op0=OP.is_equal),
                                 waits=(([tok_route] if (e, g2) == (0, 0)
                                         else []) +
                                        ([ohread[par]] if t2 == 0 and
                                         ohread[par] is not None else [])),
                                 inc="sv" if t2 == 1 else None, drain=False)
                    ohtok[par] = oht
                    for dc in range(8):
                        dpar = dc % 2
                        w = [xl, ohtok[par]]
                        if dev_tok[dpar] is not None:
                            w.append(dev_tok[dpar])
                        if (g2, dc) == (0, 0) and e == 0:
                            w.append(b1tok[E - 1])
                        if (g2, dc) == (0, 1) and oevh[0] is not None:
                            w.append(oevh[0])

                        def dmm(t, g2=g2, dc=dc, par=par, dpar=dpar):
                            r = None
                            for t2 in range(2):
                                r = t.matmul(
                                    ps_f[:, DBASE[dpar] + t2 * CAPT:
                                         DBASE[dpar] + (t2 + 1) * CAPT],
                                    xnb[:, par * 2048 + t2 * D + dc * 128:
                                        par * 2048 + t2 * D + dc * 128 + 128],
                                    ohb[:, par * 2 * CAPT + t2 * CAPT:
                                        par * 2 * CAPT + (t2 + 1) * CAPT],
                                    start=True, stop=True)
                            return r
                        dm = EM("tensor", dmm, waits=w, inc="st")
                        xnread[par] = dm
                        ohread[par] = dm
                        ew = [dm]
                        if (g2, dc) == (0, 0) and xgt_read is not None:
                            ew.append(xgt_read)
                        dev_tok[dpar] = EM(
                            "vector", lambda v, g2=g2, dc=dc, dpar=dpar:
                            v.tensor_copy(
                                xgt[:, dc * SPE + g2 * 2 * CAPT:
                                    dc * SPE + (g2 + 1) * 2 * CAPT],
                                ps_f[:, DBASE[dpar]:DBASE[dpar] + 2 * CAPT]),
                            waits=ew, inc="sv", drain=False)
                disp_done = dev_tok[1]

                for q in range(4):
                    k = 4 * e + q
                    kpar = k % 2
                    hm = None
                    for fb in range(8):
                        for dc in range(8):
                            w = []
                            if dc == 0:
                                w += [t for t in hread if t is not None]
                                if fb == 0:
                                    w += [wtok[k], disp_done]

                            def hmm(t, fb=fb, dc=dc, kpar=kpar):
                                r = None
                                for sb in range(len(HSB)):
                                    r = t.matmul(
                                        ps_f[:, HOFF[sb]:HOFF[sb] + HSB[sb]],
                                        wf1[kpar][:, dc * 1024 + fb * 128:
                                                  dc * 1024 + fb * 128 + 128],
                                        xgt[:, dc * SPE + HOFF[sb]:
                                            dc * SPE + HOFF[sb] + HSB[sb]],
                                        start=(dc == 0), stop=(dc == 7))
                                return r
                            hm = EM("tensor", hmm, waits=w, inc="st")
                        for sb in range(len(HSB)):
                            gw = [hm]
                            if (fb, sb) == (0, 0):
                                gw.append(b1tok[e])
                                if w2read is not None:
                                    gw.append(w2read)
                            hread[sb] = EM(
                                "scalar", lambda sc, fb=fb, sb=sb, e=e, q=q:
                                sc.activation(
                                    hsb[:, fb * SPE + HOFF[sb]:
                                        fb * SPE + HOFF[sb] + HSB[sb]],
                                    ps_f[:, HOFF[sb]:HOFF[sb] + HSB[sb]],
                                    AF.Gelu,
                                    bias=b1c[:, e * 32 + q * 8 + fb:
                                             e * 32 + q * 8 + fb + 1]),
                                waits=gw, inc="ss", drain=False)
                    xgt_read = hm
                    glast = hread[len(HSB) - 1]
                    if k + 2 < 4 * E:
                        EM("gpsimd", lambda g, k=k, kpar=kpar: w1load(
                            g, (k + 2) // 4, (k + 2) % 4, kpar),
                            waits=[hm], inc="dwf%d" % kpar, amt=16)
                        wtok[k + 2] = T("dwf%d" % kpar)

                    for s in range(NSB):
                        for hf, base in ((0, OA), (1, OB)):
                            w = []
                            if (s, hf) == (0, 0):
                                w += [w2tok[k], glast]
                            if oevh[hf] is not None:
                                w.append(oevh[hf])

                            def omm(t, s=s, hf=hf, base=base):
                                r = None
                                for fb in range(8):
                                    r = t.matmul(
                                        ps_f[:, base:base + 512],
                                        hsb[:, fb * SPE + s * 128:
                                            fb * SPE + s * 128 + 128],
                                        wf2s[:, fb * 1024 + hf * 512:
                                             fb * 1024 + (hf + 1) * 512],
                                        start=(fb == 0), stop=(fb == 7))
                                return r
                            om = EM("tensor", omm, waits=w, inc="st")
                            ew = [om]
                            if q == 0 and (s, hf) == (0, 0) and \
                                    oacc_flush is not None:
                                ew.append(oacc_flush)
                            dst = (s * D + hf * 512, s * D + (hf + 1) * 512)
                            if q == 0:
                                oevh[hf] = EM(
                                    "vector", lambda v, dst=dst, base=base:
                                    v.tensor_copy(
                                        oacc[:, dst[0]:dst[1]],
                                        ps_f[:, base:base + 512]),
                                    waits=ew, inc="sv", drain=False)
                            else:
                                oevh[hf] = EM(
                                    "vector", lambda v, dst=dst, base=base:
                                    v.tensor_tensor(
                                        oacc[:, dst[0]:dst[1]],
                                        oacc[:, dst[0]:dst[1]],
                                        ps_f[:, base:base + 512],
                                        op=OP.add),
                                    waits=ew, inc="sv", drain=False)
                    w2read = om
                    if k + 1 < 4 * E:
                        EM("gpsimd", lambda g, k=k: w2load(
                            g, (k + 1) // 4, (k + 1) % 4),
                            waits=[om], inc="dtb", amt=16)
                        w2tok[k + 1] = T("dtb")

                oacc_flush = EM("gpsimd", lambda g, e=e: g.dma_start(
                    out=o_dram[e * SPE:(e + 1) * SPE, :]
                    .rearrange("(sb p) d -> p sb d", p=128),
                    in_=oacc[:].rearrange("p (sb d) -> p sb d", d=D)),
                    waits=[oevh[0], oevh[1]], inc="dsc", amt=16)
                o_flush_tok[e] = oacc_flush
            P.flush(block)

        P.fence()
        with contextlib.ExitStack() as stC:
            sbC = lambda *a: stC.enter_context(nc.sbuf_tensor(*a))
            obuf = sbC("obuf", [CAPT, 2 * E * D], BF16)
            ytb = sbC("ytb", [128, 2 * D], F32)
            ysb = sbC("ysb", [128, 2 * D], F32)
            ohc = sbC("ohc", [128, CAPT], BF16)
            ohgc = sbC("ohgc", [128, CAPT], F32)
            ohgT = sbC("ohgT", [CAPT, 16 * 128], BF16)
            gTt = sbC("gTt", [8, 2 * 128], BF16)
            ps_y = stC.enter_context(nc.psum_tensor("ps_y", [128, 2048], F32))
            ps_tr = stC.enter_context(nc.psum_tensor("ps_tr", [128, 512], F32))

            ytread = [None, None]
            obread = [None, None]
            ohg_read = [None, None]
            trev = [None, None]
            prev_tr = None
            prev_gev = None
            ywr_tok = [None, None]
            pend_fin = [None]

            def finalize_tile(i):
                par = i % 2
                mml, ytl = fin_state[par]
                tts = None
                for f in range(2):
                    tw = []
                    if f == 0:
                        tw = [mml, ytl]
                        if ywr_tok[par] is not None:
                            tw.append(ywr_tok[par])
                    tts = EM("vector", lambda v, par=par, f=f: v.tensor_tensor(
                        ysb[:, par * D + f * 512:par * D + (f + 1) * 512],
                        ps_y[:, par * 1024 + f * 512:
                             par * 1024 + (f + 1) * 512],
                        ytb[:, par * D + f * 512:par * D + (f + 1) * 512],
                        op=OP.add), waits=tw, inc="sv" if f == 1 else None,
                        drain=False)
                ytread[par] = tts
                ywr_tok[par] = EM("sync", lambda s, i=i, par=par: s.dma_start(
                    out=y_out[i * 128:(i + 1) * 128, :],
                    in_=ysb[:, par * D:(par + 1) * D]),
                    waits=[tts], inc="dyw%d" % par, amt=16)

            fin_state = [None, None]
            for i in range(NT):
                par = i % 2
                ytl = EM("sync", lambda s, i=i, par=par: s.dma_start(
                    out=ytb[:, par * D:(par + 1) * D],
                    in_=y_out[i * 128:(i + 1) * 128, :]),
                    waits=(tok_y_all if i < 2 else []) +
                    ([ytread[par]] if ytread[par] is not None else []),
                    inc="dmx%d" % par, amt=16)
                obl = []
                for e2 in range(E):
                    obl.append(EM("sync", lambda s, i=i, par=par, e2=e2:
                                  s.dma_start(
                                      out=obuf[:, par * 8192 + e2 * D:
                                               par * 8192 + (e2 + 1) * D],
                                      in_=o_dram[e2 * SPE + i * CAPT:
                                                 e2 * SPE + (i + 1) * CAPT, :]),
                                  waits=([o_flush_tok[e2]] if i < 2 else []) +
                                  ([obread[par]] if e2 == 0 and
                                   obread[par] is not None else []),
                                  inc="dgx%d" % par, amt=16))
                gtr = EM("tensor", lambda t, i=i: t.transpose(
                    ps_tr[0:8, 256:384], gall[:, i * E:(i + 1) * E],
                    identf[:]),
                    waits=[prev_gev] + ([ohg_read[par]]
                                        if ohg_read[par] is not None else []),
                    inc="st")
                gev = EM("vector", lambda v, par=par: v.tensor_copy(
                    gTt[:, par * 128:(par + 1) * 128], ps_tr[0:8, 256:384]),
                    waits=[gtr], inc="sv", drain=False)
                prev_gev = gev

                ev8 = []
                for e2 in range(E):
                    pe = e2 % 2
                    oh2 = EM("vector", lambda v, i=i, e2=e2: v.tensor_scalar(
                        ohgc[:], iota_j[:, 0:CAPT],
                        jall[:, i * E + e2:i * E + e2 + 1],
                        gall[:, i * E + e2:i * E + e2 + 1],
                        op0=OP.is_equal, op1=OP.mult),
                        waits=[prev_tr] if prev_tr is not None else [],
                        inc="sv", drain=False)
                    tr = EM("tensor", lambda t, pe=pe: t.transpose(
                        ps_tr[0:CAPT, pe * 128:(pe + 1) * 128], ohgc[:],
                        identf[:]), waits=[oh2] +
                        ([trev[pe]] if trev[pe] is not None else []),
                        inc="st")
                    prev_tr = tr
                    slot = (i % 2) * 8 + e2
                    ew = [tr]
                    if e2 == 0 and ohg_read[par] is not None:
                        ew.append(ohg_read[par])
                    trev[pe] = EM("vector", lambda v, pe=pe, slot=slot:
                                  v.tensor_copy(
                                      ohgT[:, slot * 128:(slot + 1) * 128],
                                      ps_tr[0:CAPT, pe * 128:(pe + 1) * 128]),
                                  waits=ew, inc="sv", drain=False)
                    ev8.append(trev[pe])

                if i > 0:
                    finalize_tile(i - 1)

                mml = None
                for e2 in range(E):
                    slot = (i % 2) * 8 + e2
                    w = [ev8[e2], obl[e2]]
                    if e2 == 0:
                        w += [gev, tok_b2]
                        if ytread[par] is not None:
                            w.append(ytread[par])

                    def cmb(t, e2=e2, par=par, slot=slot):
                        r = None
                        if e2 == 0:
                            for f in range(2):
                                r = t.matmul(
                                    ps_y[:, par * 1024 + f * 512:
                                         par * 1024 + (f + 1) * 512],
                                    gTt[:, par * 128:(par + 1) * 128],
                                    b2all[:, f * 512:(f + 1) * 512],
                                    start=True, stop=False)
                        for f in range(2):
                            r = t.matmul(
                                ps_y[:, par * 1024 + f * 512:
                                     par * 1024 + (f + 1) * 512],
                                ohgT[:, slot * 128:(slot + 1) * 128],
                                obuf[:, par * 8192 + e2 * D + f * 512:
                                     par * 8192 + e2 * D + (f + 1) * 512],
                                start=False, stop=(e2 == E - 1))
                        return r
                    mml = EM("tensor", cmb, waits=w, inc="st")
                    obread[par] = mml
                ohg_read[par] = mml
                fin_state[par] = (mml, ytl)
            finalize_tile(NT - 1)

            EM("gpsimd", lambda g: g.nop(),
               waits=[T("dyw0"), T("dyw1"), T("dsn0"), T("dsn1"), T("dsc")])
            P.flush(block)

    nc.finalize()
    return nc


_NC_CACHE = [None]


def kernel(**inputs):
    x = np.ascontiguousarray(np.asarray(inputs["L0_attn_x"], dtype=np.float32))
    wq = np.ascontiguousarray(np.asarray(inputs["L0_attn_Wq"], dtype=np.float32))
    wk = np.ascontiguousarray(np.asarray(inputs["L0_attn_Wk"], dtype=np.float32))
    wv = np.ascontiguousarray(np.asarray(inputs["L0_attn_Wv"], dtype=np.float32))
    wo = np.ascontiguousarray(np.asarray(inputs["L0_attn_Wo"], dtype=np.float32))
    wr = np.ascontiguousarray(np.asarray(inputs["L0_ffn_Wr"], dtype=np.float32))
    br = np.ascontiguousarray(np.asarray(inputs["L0_ffn_br"], dtype=np.float32))
    w1 = np.ascontiguousarray(np.asarray(inputs["L0_ffn_W1"], dtype=np.float32))
    b1 = np.ascontiguousarray(np.asarray(inputs["L0_ffn_b1"], dtype=np.float32))
    w2 = np.ascontiguousarray(np.asarray(inputs["L0_ffn_W2"], dtype=np.float32))
    b2 = np.ascontiguousarray(np.asarray(inputs["L0_ffn_b2"], dtype=np.float32))

    cosq, sinq, _, _ = _host_tables()
    xf = x.reshape(B * L, D)

    if _NC_CACHE[0] is None:
        _NC_CACHE[0] = build_nc()
    nc = _NC_CACHE[0]

    in_maps = []
    for c in range(NCORES):
        pm = np.zeros((E, 1), np.float32)
        pm[:c, 0] = 1.0
        in_maps.append({
            "x": xf[c * TPC:(c + 1) * TPC],
            "wq": wq, "wk": wk, "wv": wv, "wo": wo,
            "wr": wr, "br": br.reshape(1, E),
            "w1": w1, "b1": b1, "w2": w2, "b2": b2,
            "cosq": cosq, "sinq": sinq,
            "prevmask": pm,
        })
    res = run_bass_kernel_spmd(nc, in_maps, core_ids=list(range(NCORES)))
    out = np.concatenate([res.results[c]["y"] for c in range(NCORES)], axis=0)
    return out.reshape(B, L, D)


# version 08
# speedup vs baseline: 1.1034x; 1.1034x over previous
import math
import contextlib
import numpy as np
import ml_dtypes
import concourse.bass as bass
import concourse.mybir as mybir
from concourse.alu_op_type import AluOpType as OP
from concourse.bass_utils import run_bass_kernel_spmd

F32 = mybir.dt.float32
BF16 = mybir.dt.bfloat16
U32 = mybir.dt.uint32
I16 = mybir.dt.int16
AF = mybir.ActivationFunctionType
AX = mybir.AxisListType.X

NCORES = 8
B, L, D = 16, 2048, 1024
TPC = 4096
NT = 32
H, KV, HD = 16, 4, 64
E = 8
CAPG = 8192
CAPT = 72
SPE = NT * CAPT
NSLOT = E * SPE
NSB = SPE // 128
HSB = [512, 512, 512, 512, 256]
DFF = 4096
C0 = 24.0
EPS = 1e-5
ACUT = 36.0


def _kcmin(h):
    import math as _m
    s = 2.0 ** (-(2.0 ** -(_m.log2(H) - 3)))
    sl = s * (s ** h)
    kc = 0
    while sl * (1920.0 - 128.0 * kc) > ACUT:
        kc += 1
    return kc
ENGS = ("sync", "tensor", "vector", "scalar", "gpsimd")
DRAIN_ENGS = ("vector", "scalar", "gpsimd")


def _slopes():
    start = 2.0 ** (-(2.0 ** -(math.log2(H) - 3)))
    return [start * (start ** i) for i in range(H)]


def _bf(v):
    return float(ml_dtypes.bfloat16(v))


def _host_tables():
    inv = 1.0 / (10000.0 ** (np.arange(0, HD, 2, dtype=np.float64) / HD))
    t = np.arange(L, dtype=np.float64)
    fr = inv[:, None] * t[None, :]
    cosT = np.concatenate([np.cos(fr), np.cos(fr)], 0)
    sinT = np.concatenate([np.sin(fr), np.sin(fr)], 0)
    sgn = np.concatenate([-np.ones((32, 1)), np.ones((32, 1))], 0)
    sinS = sinT * sgn
    cosq = np.concatenate([cosT / 8.0, cosT / 8.0], 0)
    sinq = np.concatenate([sinS / 8.0, sinS / 8.0], 0)
    return (cosq.astype(np.float32), sinq.astype(np.float32),
            cosT.astype(np.float32), sinS.astype(np.float32))


class Prog:
    def __init__(self):
        self.q = {k: [] for k in ENGS}
        self.cnt = {}
        self.sems = None
        self.pending = {k: [] for k in ENGS}

    def tok(self, sem):
        return (sem, self.cnt.get(sem, 0))

    def fence(self):
        toks = [(s, v) for s, v in self.cnt.items() if v > 0]
        for k in ENGS:
            self.pending[k] = list(toks)

    DMA_SEMS = ("dmx0", "dmx1", "dsn0", "dsn1", "dtr", "dcc", "dwa", "dtb",
                "dwf0", "dwf1", "dgx0", "dgx1", "dro", "dsc", "dyw0", "dyw1",
                "daug")

    def emit(self, eng, fn, waits=(), inc=None, amt=1, drain=None):
        va = None
        selfwait = None
        if inc is not None:
            if inc in self.DMA_SEMS:
                prev = self.cnt.get(inc, 0)
                if prev > 0:
                    selfwait = (inc, prev)
            self.cnt[inc] = self.cnt.get(inc, 0) + amt
            va = self.cnt[inc]
        allw = [w for w in self.pending[eng] if w is not None]
        self.pending[eng] = []
        allw += [w for w in waits if w is not None]
        if selfwait is not None:
            allw.append(selfwait)
        allw = tuple((s, v) for (s, v) in allw if v > 0)
        do_drain = (eng in DRAIN_ENGS) if drain is None else drain

        def item(h, self=self, allw=allw, inc=inc, amt=amt, fn=fn,
                 do_drain=do_drain):
            seen = {}
            for (s, v) in allw:
                if seen.get(s, -1) < v:
                    h.wait_ge(self.sems[s], v)
                    seen[s] = v
            r = fn(h)
            if inc is not None:
                assert r is not None
                r.then_inc(self.sems[inc], amt)
            if do_drain:
                h.drain()

        self.q[eng].append(item)
        return (inc, va) if inc is not None else None

    def flush(self, block):
        for eng in ENGS:
            items, self.q[eng] = self.q[eng], []
            if not items:
                continue

            def _run(h, items=items):
                for it in items:
                    it(h)
            getattr(block, eng)(_run)


def build_nc():
    nc = bass.Bass("TRN2", target_bir_lowering=False, debug=False)

    x_in = nc.dram_tensor("x", [TPC, D], F32, kind="ExternalInput").ap()
    wq_in = nc.dram_tensor("wq", [D, D], F32, kind="ExternalInput").ap()
    wk_in = nc.dram_tensor("wk", [D, 256], F32, kind="ExternalInput").ap()
    wv_in = nc.dram_tensor("wv", [D, 256], F32, kind="ExternalInput").ap()
    wo_in = nc.dram_tensor("wo", [D, D], F32, kind="ExternalInput").ap()
    wr_in = nc.dram_tensor("wr", [D, E], F32, kind="ExternalInput").ap()
    br_in = nc.dram_tensor("br", [1, E], F32, kind="ExternalInput").ap()
    w1_in = nc.dram_tensor("w1", [E, D, DFF], F32, kind="ExternalInput").ap()
    b1_in = nc.dram_tensor("b1", [E, DFF], F32, kind="ExternalInput").ap()
    w2_in = nc.dram_tensor("w2", [E, DFF, D], F32, kind="ExternalInput").ap()
    b2_in = nc.dram_tensor("b2", [E, D], F32, kind="ExternalInput").ap()
    cosq_in = nc.dram_tensor("cosq", [128, L], F32, kind="ExternalInput").ap()
    sinq_in = nc.dram_tensor("sinq", [128, L], F32, kind="ExternalInput").ap()
    pmask_in = nc.dram_tensor("prevmask", [E, 1], F32, kind="ExternalInput").ap()
    y_out = nc.dram_tensor("y", [TPC, D], F32, kind="ExternalOutput").ap()

    import os as _os
    _dbg = dict(kind="ExternalOutput") if _os.environ.get("KDBG") else {}
    xn_hbm = nc.dram_tensor("xn_hbm", [TPC + 1, D], BF16, **_dbg).ap()
    o_dram = nc.dram_tensor("o_dram", [NSLOT, D], BF16, **_dbg).ap()
    cc_in = nc.dram_tensor("cc_in", [1, E], F32).ap()
    cc_out = nc.dram_tensor("cc_out", [NCORES, E], F32, **_dbg).ap()
    augd = nc.dram_tensor("augd", [7, 16], F32).ap()

    slopes = _slopes()
    P = Prog()
    T = P.tok
    EM = P.emit

    sem_names = ["dmx0", "dmx1", "dsn0", "dsn1", "dtr", "dcc", "dwa", "dtb",
                 "dwf0", "dwf1", "dgx0", "dgx1", "dro", "dsc", "dyw0", "dyw1",
                 "daug", "st", "sv", "ss", "sg"]

    with contextlib.ExitStack() as stack:
        sb = lambda *a: stack.enter_context(nc.sbuf_tensor(*a))
        sems = {n: stack.enter_context(nc.semaphore(n)) for n in sem_names}
        P.sems = sems

        ones_bf = sb("ones_bf", [128, 128], BF16)
        constp = sb("constp", [128, 8], F32)
        identf = sb("identf", [128, 128], F32)
        iota_p = sb("iota_p", [128, 1], F32)
        iota_j = sb("iota_j", [128, 128], F32)
        crossp = sb("crossp", [1, E], F32)
        ball_sb = sb("ball", [E, E], F32)
        pmask_sb = sb("pmask_sb", [E, 1], F32)
        wr_sb = sb("wr_sb", [128, 8 * E], BF16)
        br_sb = sb("br_sb", [1, E], BF16)
        jall = sb("jall", [128, NT * E], F32)
        gall = sb("gall", [128, NT * E], F32)
        b2all = sb("b2all", [E, D], BF16)

        block = stack.enter_context(nc.Block())

        for (dst, pat, cm) in (
            (iota_p[:], [[0, 1]], 1),
            (iota_j[:], [[1, 128]], 0),
        ):
            EM("gpsimd", lambda g, dst=dst, pat=pat, cm=cm: g.iota(
                dst, pattern=pat, base=0, channel_multiplier=cm,
                allow_small_or_imprecise_dtypes=True), inc="sg")
        tok_iota = T("sg")
        for (dst, src) in ((wr_sb[:].rearrange("p (c e) -> p c e", e=E),
                            wr_in[:].rearrange("(c p) e -> p c e", p=128)),
                           (br_sb[:], br_in[:]), (pmask_sb[:], pmask_in[:])):
            EM("gpsimd", lambda g, dst=dst, src=src: g.dma_start(out=dst, in_=src),
               inc="dwa", amt=16)
        tok_smallw = T("dwa")

        EM("vector", lambda v: v.memset(constp[:, 0:1], EPS))
        nc.const_aps.register(F32, EPS, constp[:, 0:1])
        EM("vector", lambda v: v.memset(constp[:, 1:2], -0.5))
        nc.const_aps.register(F32, -0.5, constp[:, 1:2])
        EM("vector", lambda v: v.memset(constp[:, 2:3], 0.0))
        EM("vector", lambda v: v.memset(ones_bf[:], 1.0))
        EM("vector", lambda v: v.memset(zrow[:], 0.0))
        EM("vector", lambda v: v.tensor_scalar(
            identf[:], iota_j[:], iota_p[:], None, op0=OP.is_equal),
            waits=[tok_iota], inc="sv")
        tok_consts = T("sv")

        with contextlib.ExitStack() as st1:
            sb1 = lambda *a: st1.enter_context(nc.sbuf_tensor(*a))
            zrow = sb1("zrow", [1, D], BF16)
            xbuf = sb1("xbuf", [128, 2 * D], F32)
            xcbuf = sb1("xcbuf", [128, 2 * D], F32)
            xnbuf = sb1("xnbuf", [128, 2 * D], BF16)
            stats = sb1("stats", [128, 16], F32)
            EM("sync", lambda s: s.dma_start(out=xn_hbm[TPC:TPC + 1, :],
                                             in_=zrow[:]),
               waits=[tok_consts], inc="dtr", amt=16)

            prev_xn = [None, None]
            for i in range(NT):
                par = i % 2
                so = 8 * par
                ld = EM("sync", lambda s, i=i, par=par: s.dma_start(
                    out=xbuf[:, par * D:(par + 1) * D],
                    in_=x_in[i * 128:(i + 1) * 128, :]),
                    waits=[prev_xn[par]], inc="dmx%d" % par, amt=16)
                xt = xbuf[:, par * D:(par + 1) * D]
                xct = xcbuf[:, par * D:(par + 1) * D]
                xnt = xnbuf[:, par * D:(par + 1) * D]
                EM("vector", lambda v, xt=xt, so=so: v.reduce_sum(
                    stats[:, so:so + 1], xt, axis=AX), waits=[ld])
                EM("vector", lambda v, so=so: v.tensor_scalar(
                    stats[:, so + 2:so + 3], stats[:, so:so + 1], -1.0 / D, None,
                    op0=OP.mult))
                EM("vector", lambda v, so=so: v.tensor_tensor(
                    stats[:, so + 3:so + 4], stats[:, so + 2:so + 3],
                    stats[:, so + 2:so + 3], op=OP.mult))
                EM("vector", lambda v, xt=xt, xct=xct: v.tensor_tensor(
                    xct, xt, xt, op=OP.mult))
                EM("vector", lambda v, xct=xct, so=so: v.reduce_sum(
                    stats[:, so + 1:so + 2], xct, axis=AX))
                stt = EM("vector", lambda v, so=so: v.scalar_tensor_tensor(
                    stats[:, so + 4:so + 5], stats[:, so + 1:so + 2], 1.0 / D,
                    stats[:, so + 3:so + 4], op0=OP.mult, op1=OP.subtract),
                    inc="sv")
                EM("scalar", lambda sc, so=so: sc.activation(
                    stats[:, so + 5:so + 6], stats[:, so + 4:so + 5], AF.Ln,
                    bias=EPS), waits=[stt])
                rstd_tok = EM("scalar", lambda sc, so=so: sc.activation(
                    stats[:, so + 7:so + 8], stats[:, so + 5:so + 6], AF.Exp,
                    scale=-0.5), inc="ss")
                xn_done = EM("vector", lambda v, xnt=xnt, xt=xt, so=so:
                             v.tensor_scalar(
                                 xnt, xt, stats[:, so + 2:so + 3],
                                 stats[:, so + 7:so + 8],
                                 op0=OP.add, op1=OP.mult),
                             waits=[rstd_tok], inc="sv")
                prev_xn[par] = EM("sync", lambda s, i=i, xnt=xnt: s.dma_start(
                    out=xn_hbm[i * 128:(i + 1) * 128, :], in_=xnt),
                    waits=[xn_done], inc="dsn%d" % par, amt=16)
            tok_xn_all = [T("dsn0"), T("dsn1")]
            P.flush(block)

        P.fence()
        with contextlib.ExitStack() as stA:
            sbA = lambda *a: stA.enter_context(nc.sbuf_tensor(*a))
            xnT = sbA("xnT", [128, 8 * TPC], BF16)
            wqo_sb = sbA("wqo_sb", [128, 8 * D], BF16)
            wkv_sb = sbA("wkv_sb", [128, 8 * 512], BF16)
            cosq_sb = sbA("cosq_sb", [128, L], BF16)
            sinq_sb = sbA("sinq_sb", [128, L], BF16)
            qaug0 = sbA("qaug0", [71, L], BF16)
            qaug1 = sbA("qaug1", [71, L], BF16)
            qaug2 = sbA("qaug2", [71, L], BF16)
            qaug3 = sbA("qaug3", [71, L], BF16)
            kaug0 = sbA("kaug0", [71, L], BF16)
            kaug1 = sbA("kaug1", [71, L], BF16)
            kaug2 = sbA("kaug2", [71, L], BF16)
            kaug3 = sbA("kaug3", [71, L], BF16)
            augcolT = sbA("augcolT", [71, 16], F32)
            vaug = sbA("vaug", [128, 16 * 128], BF16)
            attno = sbA("attno", [128, 8 * L], BF16)
            ptil = sbA("ptil", [128, 2 * 1024], BF16)
            rtmp = sbA("rtmp", [128, L], F32)
            lgtop = sbA("lgtop", [128, 8], F32)
            gtmp = sbA("gtmp", [128, 4], F32)
            qaug = [qaug0, qaug1, qaug2, qaug3]
            kaug = [kaug0, kaug1, kaug2, kaug3]

            for cch in range(8):
                EM("sync", lambda s, cch=cch: s.dma_start_transpose(
                    xnT[:, cch * TPC:(cch + 1) * TPC],
                    xn_hbm[0:TPC, cch * 128:(cch + 1) * 128]),
                    waits=tok_xn_all if cch == 0 else [], inc="dtr", amt=16)
            tok_xnT = T("dtr")

            for cch in range(8):
                EM("gpsimd", lambda g, cch=cch: g.dma_start(
                    out=wqo_sb[:, cch * D:(cch + 1) * D],
                    in_=wq_in[cch * 128:(cch + 1) * 128, :]), inc="dwa", amt=16)
            for cch in range(8):
                EM("gpsimd", lambda g, cch=cch: g.dma_start(
                    out=wkv_sb[:, cch * 512:cch * 512 + 256],
                    in_=wk_in[cch * 128:(cch + 1) * 128, :]), inc="dwa", amt=16)
                EM("gpsimd", lambda g, cch=cch: g.dma_start(
                    out=wkv_sb[:, cch * 512 + 256:cch * 512 + 512],
                    in_=wv_in[cch * 128:(cch + 1) * 128, :]), inc="dwa", amt=16)
            for dst, src in ((cosq_sb[:], cosq_in[:]), (sinq_sb[:], sinq_in[:])):
                EM("gpsimd", lambda g, dst=dst, src=src: g.dma_start(
                    out=dst, in_=src), inc="dwa", amt=16)
            tok_attw = T("dwa")

            for kv in range(KV):
                EM("vector", lambda v, kv=kv: v.memset(kaug[kv][64:71, :], 0.0))
            for s_ in range(4):
                EM("vector", lambda v, s_=s_: v.memset(qaug[s_][64:71, :], 0.0))
            for kc in range(16):
                EM("vector", lambda v, kc=kc: v.memset(
                    vaug[:, kc * 128 + 64:kc * 128 + 128], 1.0))
            tok_augs = EM("vector", lambda v: v.memset(
                vaug[0:1, 0:1], 1.0), inc="sv")
            tok_augs = T("sv")

            with contextlib.ExitStack() as st2:
                sb2 = lambda *a: st2.enter_context(nc.sbuf_tensor(*a))
                U0 = sb2("U0", [128, 128], BF16)
                selbf = sb2("selbf", [128, NT * E], BF16)
                selff = sb2("selff", [128, NT * E], F32)
                g1c = sb2("g1c", [128, NT], F32)
                brow = sb2("brow", [1, 4 * NT * E], F32)
                bgl_b = sb2("bgl_b", [128, NT * E], F32)
                sltmp = st2.enter_context(nc.sbuf_tensor("sltmp", [128, 2 * NT * E], F32))
                ps_lg = st2.enter_context(nc.psum_tensor("ps_lg", [128, 512], F32))
                ps_rw = st2.enter_context(nc.psum_tensor("ps_rw", [128, 512], F32))
                ps_ro = st2.enter_context(nc.psum_tensor("ps_ro", [128, 512], F32))
                prev_stage = None
                for which, rows in (("hi", (64, 66, 68)), ("lo", (65, 67, 69)),
                                    ("one", (70,))):
                    if which == "one":
                        stg = EM("vector", lambda v: v.memset(
                            ptil[0:1, 0:L], 1.0), waits=[prev_stage], inc="sv")
                    else:
                        pat = ([[1, 128], [0, 16]] if which == "hi"
                               else [[0, 128], [1, 16]])
                        stg = EM("gpsimd", lambda g, pat=pat: g.iota(
                            ptil[0:1, 0:L].rearrange("a (h l) -> a h l", l=16),
                            pattern=pat, base=0, channel_multiplier=0,
                            allow_small_or_imprecise_dtypes=True),
                            waits=[prev_stage], inc="sg")
                    last = None
                    for kv in range(KV):
                        for r in rows:
                            last = EM("sync", lambda s, kv=kv, r=r: s.dma_start(
                                out=kaug[kv][r:r + 1, :], in_=ptil[0:1, 0:L]),
                                waits=[stg, tok_augs], inc="dtr", amt=16)
                    prev_stage = last
                tok_krows = prev_stage
                vrow = None
                for h in range(H):
                    s0v = slopes[h]
                    hi_ = _bf(s0v)
                    mid_ = _bf(s0v - hi_)
                    lo_ = _bf(s0v - hi_ - mid_)
                    r70_ = _bf(-(C0 + 2047.0 * s0v))
                    for r, vv in enumerate([16.0 * hi_, hi_, 16.0 * mid_, mid_,
                                            16.0 * lo_, lo_, r70_]):
                        vrow = EM("vector", lambda v, r=r, h=h, vv=vv: v.memset(
                            rtmp[0:1, r * 16 + h:r * 16 + h + 1], vv))
                vrow = EM("vector", lambda v: v.tensor_copy(
                    rtmp[0:1, 112:113], rtmp[0:1, 0:1]), inc="sv")
                aw = EM("sync", lambda s: s.dma_start(
                    out=augd[:].rearrange("r h -> (r h)")[None, :],
                    in_=rtmp[0:1, 0:112]), waits=[vrow], inc="dtr", amt=16)
                ar = EM("sync", lambda s: s.dma_start(
                    out=augcolT[64:71, :], in_=augd[:]), waits=[aw],
                    inc="dtr", amt=16)
                EM("vector", lambda v: v.tensor_scalar(
                    U0[:], iota_j[:], iota_p[:], None, op0=OP.is_gt),
                    waits=[tok_iota], inc="sv")
                tok_u0 = T("sv")
                rt_tok = [None, None]
                for i in range(NT):
                    par = i % 2
                    pslice = ps_lg[:, par * 8:par * 8 + 8]

                    def lgmm(t, i=i, pslice=pslice):
                        for cch in range(8):
                            t.matmul(pslice,
                                     xnT[:, cch * TPC + i * 128:
                                         cch * TPC + (i + 1) * 128],
                                     wr_sb[:, cch * E:(cch + 1) * E],
                                     start=(cch == 0), stop=False)
                        return t.matmul(pslice, ones_bf[0:1, :], br_sb[:],
                                        start=False, stop=True)
                    mm = EM("tensor", lgmm,
                            waits=[tok_xnT, tok_smallw, rt_tok[par]], inc="st")
                    EM("vector", lambda v, pslice=pslice: v.max(
                        out=lgtop[:], in_=pslice), waits=[mm])
                    nv1 = EM("vector", lambda v: v.tensor_scalar(
                        gtmp[:, 0:1], lgtop[:, 0:1], -1.0, None, op0=OP.mult),
                        inc="sv")
                    ex = EM("scalar", lambda sc: sc.activation(
                        gtmp[:, 1:2], lgtop[:, 1:2], AF.Exp, bias=gtmp[:, 0:1]),
                        waits=[nv1], inc="ss")
                    exg = EM("scalar", lambda sc, i=i, pslice=pslice:
                             sc.activation(
                                 gall[:, i * E:(i + 1) * E], pslice, AF.Exp,
                                 bias=gtmp[:, 0:1]), inc="ss")
                    EM("vector", lambda v, i=i, pslice=pslice: v.tensor_scalar(
                        selff[:, i * E:(i + 1) * E], pslice, lgtop[:, 1:2], None,
                        op0=OP.is_ge))
                    EM("vector", lambda v, i=i: v.tensor_copy(
                        selbf[:, i * E:(i + 1) * E], selff[:, i * E:(i + 1) * E]))
                    EM("vector", lambda v: v.tensor_scalar(
                        gtmp[:, 2:3], gtmp[:, 1:2], 1.0, None, op0=OP.add),
                        waits=[ex])
                    EM("vector", lambda v, i=i: v.reciprocal(
                        g1c[:, i:i + 1], gtmp[:, 2:3]))
                    rt_tok[par] = EM("vector", lambda v, i=i: v.tensor_scalar(
                        gall[:, i * E:(i + 1) * E], gall[:, i * E:(i + 1) * E],
                        g1c[:, i:i + 1], None, op0=OP.mult),
                        waits=[exg], inc="sv")
                tok_rout = T("sv")

                cmm = EM("tensor", lambda t: t.matmul(
                    ps_ro[0:1, 0:NT * E], ones_bf[:, 0:1], selbf[:],
                    start=True, stop=True), waits=[tok_rout], inc="st")
                rwm = EM("tensor", lambda t: t.matmul(
                    ps_rw[:, 0:NT * E], U0[:], selbf[:], start=True, stop=True),
                    waits=[tok_u0], inc="st")
                EM("vector", lambda v: v.tensor_copy(
                    brow[:, 0:NT * E], ps_ro[0:1, 0:NT * E]), waits=[cmm])
                for e in range(E):
                    EM("vector", lambda v, e=e: v.tensor_tensor_scan(
                        brow[:, NT * E + e:2 * NT * E:E], brow[:, e:NT * E:E],
                        brow[:, e:NT * E:E], 0.0, op0=OP.add, op1=OP.bypass))
                bv = EM("vector", lambda v: v.tensor_tensor(
                    brow[:, 2 * NT * E:3 * NT * E], brow[:, NT * E:2 * NT * E],
                    brow[:, 0:NT * E], op=OP.subtract), inc="sv")
                ccw = EM("sync", lambda s: s.dma_start(
                    out=cc_in[:],
                    in_=brow[:, NT * E + (NT - 1) * E:2 * NT * E]),
                    waits=[bv], inc="dcc", amt=16)
                cc = EM("gpsimd", lambda g: g.collective_compute(
                    "AllGather", OP.bypass, replica_groups=[list(range(NCORES))],
                    ins=[cc_in[:].opt()], outs=[cc_out[:].opt()]),
                    waits=[ccw], inc="sg")
                ccr = EM("sync", lambda s: s.dma_start(
                    out=ball_sb[:], in_=cc_out[:]), waits=[cc], inc="dcc", amt=16)
                cpm = EM("tensor", lambda t: t.matmul(
                    ps_ro[0:1, 64:64 + E], pmask_sb[:], ball_sb[:],
                    start=True, stop=True), waits=[ccr, tok_smallw], inc="st")
                EM("vector", lambda v: v.tensor_copy(
                    crossp[:], ps_ro[0:1, 64:64 + E]), waits=[cpm])
                for i in range(NT):
                    last_add = EM("vector", lambda v, i=i: v.tensor_tensor(
                        brow[:, 3 * NT * E + i * E:3 * NT * E + (i + 1) * E],
                        brow[:, 2 * NT * E + i * E:2 * NT * E + (i + 1) * E],
                        crossp[:], op=OP.add),
                        inc="sv" if i == NT - 1 else None)
                bases = last_add
                ones_f = sb2("ones_f", [1, 128], F32)
                of = EM("vector", lambda v: v.memset(ones_f[:], 1.0), inc="sv")
                bcm = EM("tensor", lambda t: t.matmul(
                    ps_ro[:, 256:256 + NT * E], ones_f[:],
                    brow[0:1, 3 * NT * E:4 * NT * E], start=True, stop=True),
                    waits=[bases, of], inc="st")
                bglev = EM("vector", lambda v: v.tensor_copy(
                    bgl_b[:], ps_ro[:, 256:256 + NT * E]), waits=[bcm],
                    inc="sv")

                Am = sltmp[:, 0:NT * E]
                Jt = sltmp[:, NT * E:2 * NT * E]
                EM("vector", lambda v: v.tensor_tensor(
                    Am, ps_rw[:, 0:NT * E], bgl_b[:], op=OP.add),
                    waits=[rwm, bglev, tok_augs])
                EM("vector", lambda v: v.tensor_scalar(
                    Am, Am, float(CAPG), None, op0=OP.is_lt))
                EM("vector", lambda v: v.tensor_tensor(Am, Am, selff[:], op=OP.mult))
                EM("vector", lambda v: v.tensor_tensor(
                    gall[:], gall[:], Am, op=OP.mult))
                skb = EM("vector", lambda v: v.tensor_copy(selbf[:], Am),
                         inc="sv")
                rwm2 = EM("tensor", lambda t: t.matmul(
                    ps_lg[:, 0:NT * E], U0[:], selbf[:], start=True, stop=True),
                    waits=[skb], inc="st")
                EM("vector", lambda v: v.tensor_scalar(
                    Jt, Am, -float(CAPT), float(CAPT), op0=OP.mult, op1=OP.add))
                EM("vector", lambda v: v.tensor_tensor(
                    jall[:], ps_lg[:, 0:NT * E], Jt, op=OP.add),
                    waits=[rwm2], inc="sv")
                tok_route = T("sv")
                P.flush(block)

            P.fence()
            with contextlib.ExitStack() as st4:
                xrb = st4.enter_context(nc.sbuf_tensor("xrb", [128, 2 * D], F32))
                yob = st4.enter_context(nc.sbuf_tensor("yob", [128, 2 * D], F32))
                ps_proj = st4.enter_context(nc.psum_tensor("ps_proj", [128, 2048], F32))
                ps_pv = st4.enter_context(nc.psum_tensor("ps_pv", [128, 1024], F32))
                ps_wo = st4.enter_context(nc.psum_tensor("ps_wo", [128, 1024], F32))
                lproj = None
                pvread = None
                pwread = [None, None]
                yev = [None, None]
                wo_done = None
                for b in range(2):
                    tb = b * L
                    if b == 1:
                        for cch in range(8):
                            EM("gpsimd", lambda g, cch=cch: g.dma_start(
                                out=wqo_sb[:, cch * D:(cch + 1) * D],
                                in_=wq_in[cch * 128:(cch + 1) * 128, :]),
                                waits=[wo_done] if cch == 0 else [],
                                inc="dwa", amt=16)
                    tok_wq = T("dwa")
                    for kv in range(KV):
                        for s_ in range(4):
                            h = 4 * kv + s_
                            EM("vector", lambda v, s_=s_, h=h: v.tensor_scalar(
                                qaug[s_][64:71, :], qaug[s_][64:71, :], 0.0,
                                augcolT[64:71, h:h + 1], op0=OP.mult, op1=OP.add),
                                waits=[tok_augs, lproj, T("st"),
                                       ar if (b, kv) == (0, 0) else None],
                                inc="sv" if s_ == 3 else None)
                        tok_aug_inj = T("sv")

                        for dch in range(2):
                            col0 = 4 * kv * HD + dch * 128

                            def qproj(t, col0=col0, tb=tb):
                                r = None
                                for cch in range(8):
                                    for f in range(4):
                                        r = t.matmul(
                                            ps_proj[:, f * 512:(f + 1) * 512],
                                            wqo_sb[:, cch * D + col0:
                                                   cch * D + col0 + 128],
                                            xnT[:, cch * TPC + tb + f * 512:
                                                cch * TPC + tb + (f + 1) * 512],
                                            start=(cch == 0), stop=(cch == 7))
                                return r
                            qm = EM("tensor", qproj,
                                    waits=[tok_attw if b == 0 else tok_wq,
                                           tok_xnT, lproj], inc="st")
                            for hh in range(2):
                                s_ = 2 * dch + hh
                                base = hh * 64
                                EM("vector", lambda v, s_=s_, base=base:
                                   v.tensor_tensor(
                                       qaug[s_][0:64, :],
                                       ps_proj[base:base + 64, 0:2048],
                                       cosq_sb[0:64, :], op=OP.mult),
                                   waits=[qm] if hh == 0 else [])
                                EM("vector", lambda v, base=base:
                                   v.tensor_tensor(
                                       rtmp[0:32, :],
                                       ps_proj[base + 32:base + 64, 0:2048],
                                       sinq_sb[0:32, :], op=OP.mult))
                                EM("vector", lambda v, base=base:
                                   v.tensor_tensor(
                                       rtmp[32:64, :],
                                       ps_proj[base:base + 32, 0:2048],
                                       sinq_sb[32:64, :], op=OP.mult))
                                lproj = EM("vector", lambda v, s_=s_:
                                           v.tensor_tensor(
                                               qaug[s_][0:64, :],
                                               qaug[s_][0:64, :],
                                               rtmp[0:64, :], op=OP.add),
                                           inc="sv" if hh == 1 else None)

                        def kproj(t, kv=kv, tb=tb):
                            r = None
                            for cch in range(8):
                                for f in range(4):
                                    r = t.matmul(
                                        ps_proj[0:64, f * 512:(f + 1) * 512],
                                        wkv_sb[:, cch * 512 + kv * 64:
                                               cch * 512 + kv * 64 + 64],
                                        xnT[:, cch * TPC + tb + f * 512:
                                            cch * TPC + tb + (f + 1) * 512],
                                        start=(cch == 0), stop=(cch == 7))
                            return r
                        km = EM("tensor", kproj, waits=[tok_attw, lproj], inc="st")
                        EM("vector", lambda v, kv=kv: v.tensor_tensor(
                            kaug[kv][0:64, :], ps_proj[0:64, 0:2048],
                            cosq_sb[0:64, :], op=OP.mult), waits=[km])
                        EM("vector", lambda v: v.tensor_tensor(
                            rtmp[0:32, :], ps_proj[32:64, 0:2048],
                            sinq_sb[0:32, :], op=OP.mult))
                        EM("vector", lambda v: v.tensor_tensor(
                            rtmp[32:64, :], ps_proj[0:32, 0:2048],
                            sinq_sb[32:64, :], op=OP.mult))
                        EM("vector", lambda v, kv=kv: v.tensor_tensor(
                            kaug[kv][0:64, :], kaug[kv][0:64, :], rtmp[0:64, :],
                            op=OP.add))
                        lproj = EM("vector", lambda v, kv=kv: v.tensor_scalar(
                            kaug[kv][0:64, :], kaug[kv][0:64, :], 8.0, None,
                            op0=OP.mult), inc="sv")

                        vev_pp = [None, None]
                        vtc0 = min(_kcmin(4 * kv + s_) for s_ in range(4))
                        for tc in range(vtc0, 16):
                            pslot = (ps_pv[:, 0:64] if tc % 2 == 0
                                     else ps_pv[:, 512:576])
                            w = []
                            if tc == vtc0:
                                w.append(pvread)
                            if vev_pp[tc % 2] is not None:
                                w.append(vev_pp[tc % 2])

                            def vproj(t, tc=tc, kv=kv, tb=tb, pslot=pslot):
                                r = None
                                for cch in range(8):
                                    r = t.matmul(
                                        pslot,
                                        xnT[:, cch * TPC + tb + tc * 128:
                                            cch * TPC + tb + (tc + 1) * 128],
                                        wkv_sb[:, cch * 512 + 256 + kv * 64:
                                               cch * 512 + 256 + kv * 64 + 64],
                                        start=(cch == 0), stop=(cch == 7))
                                return r
                            vm = EM("tensor", vproj, waits=w, inc="st")
                            vev_pp[tc % 2] = EM(
                                "scalar", lambda sc, tc=tc, pslot=pslot:
                                sc.activation(vaug[:, tc * 128:tc * 128 + 64],
                                              pslot, AF.Copy),
                                waits=[vm], inc="ss", drain=False)
                        tok_vaug = vev_pp[1]

                        for s_ in range(4):
                            h = 4 * kv + s_
                            hc = h // 2
                            hrow = (h % 2) * 64
                            kc0 = _kcmin(h)
                            for qh in range(2):
                                q0 = qh * 1024
                                exp_toks = []
                                pv_last = None

                                def mk_sm(kc, w, kv=kv, s_=s_, q0=q0):
                                    sbh = ps_proj[:, (kc % 2) * 1024:
                                                  (kc % 2 + 1) * 1024]

                                    def smm(t, kv=kv, s_=s_, kc=kc, q0=q0,
                                            sbh=sbh):
                                        r = None
                                        for f in range(2):
                                            r = t.matmul(
                                                sbh[:, f * 512:(f + 1) * 512],
                                                kaug[kv][:, kc * 128:
                                                         (kc + 1) * 128],
                                                qaug[s_][:, q0 + f * 512:
                                                          q0 + (f + 1) * 512],
                                                start=True, stop=True)
                                        return r
                                    return EM("tensor", smm, waits=w, inc="st")

                                sm_t = {kc0: mk_sm(
                                    kc0, [tok_aug_inj, lproj, tok_vaug])}
                                exs = {}
                                for kc in range(kc0, 16):
                                    par = kc % 2
                                    ex = EM("scalar",
                                            lambda sc, par=par:
                                            sc.activation(
                                                ptil[:, par * 1024:
                                                     (par + 1) * 1024],
                                                ps_proj[:, par * 1024:
                                                        (par + 1) * 1024],
                                                AF.Exp),
                                            waits=[sm_t[kc]], inc="ss",
                                            drain=False)
                                    exs[kc] = ex
                                    exp_toks.append(ex)
                                    if kc + 1 < 16:
                                        sm_t[kc + 1] = mk_sm(
                                            kc + 1,
                                            [exs.get(kc - 1, lproj)])
                                    wpv = [ex]
                                    if kc == kc0:
                                        wpv.append(pvread)

                                    def pvmm(t, kc=kc, par=par, kc0=kc0):
                                        r = None
                                        for f in range(2):
                                            r = t.matmul(
                                                ps_pv[:, f * 512:(f + 1) * 512],
                                                vaug[:, kc * 128:(kc + 1) * 128],
                                                ptil[:, par * 1024 + f * 512:
                                                     par * 1024 + (f + 1) * 512],
                                                start=(kc == kc0),
                                                stop=(kc == 15))
                                        return r
                                    pv_last = EM("tensor", pvmm, waits=wpv,
                                                 inc="st")
                                EM("scalar", lambda sc, hrow=hrow:
                                   sc.activation(
                                       rtmp[hrow:hrow + 64, 0:1024],
                                       ps_pv[64:128, 0:1024], AF.Ln,
                                       bias=constp[hrow:hrow + 64, 2:3]),
                                   waits=[pv_last])
                                EM("scalar", lambda sc, hrow=hrow:
                                   sc.activation(
                                       rtmp[hrow:hrow + 64, 0:1024],
                                       rtmp[hrow:hrow + 64, 0:1024], AF.Exp,
                                       scale=-1.0), inc="ss", drain=False)
                                rdone = T("ss")
                                pvread = EM(
                                    "vector",
                                    lambda v, hc=hc, hrow=hrow, q0=q0:
                                    v.tensor_tensor(
                                        attno[hrow:hrow + 64,
                                              hc * L + q0:hc * L + q0 + 1024],
                                        ps_pv[0:64, 0:1024],
                                        rtmp[hrow:hrow + 64, 0:1024],
                                        op=OP.mult), waits=[rdone], inc="sv",
                                    drain=False)
                                lproj = exp_toks[-1]

                    for cch in range(8):
                        EM("gpsimd", lambda g, cch=cch: g.dma_start(
                            out=wqo_sb[:, cch * D:(cch + 1) * D],
                            in_=wo_in[cch * 128:(cch + 1) * 128, :]),
                            waits=[T("st")] if cch == 0 else [],
                            inc="dwa", amt=16)
                    tok_wo = T("dwa")

                    for tc in range(16):
                        par = tc % 2
                        pw = ps_wo if par == 0 else ps_pv
                        w = [tok_wo] if tc == 0 else []
                        if tc == 1:
                            w.append(pvread)
                        if pwread[par] is not None:
                            w.append(pwread[par])

                        def womm(t, tc=tc, pw=pw):
                            r = None
                            for hcc in range(8):
                                for f in range(2):
                                    r = t.matmul(
                                        pw[:, f * 512:(f + 1) * 512],
                                        attno[:, hcc * L + tc * 128:
                                              hcc * L + (tc + 1) * 128],
                                        wqo_sb[:, hcc * D + f * 512:
                                               hcc * D + (f + 1) * 512],
                                        start=(hcc == 0), stop=(hcc == 7))
                            return r
                        wm = EM("tensor", womm, waits=w, inc="st")
                        xl = EM("sync", lambda s, b=b, tc=tc, par=par: s.dma_start(
                            out=xrb[:, par * D:(par + 1) * D],
                            in_=x_in[b * L + tc * 128:b * L + (tc + 1) * 128, :]),
                            waits=[pwread[par]], inc="dmx%d" % par, amt=16)
                        ye = EM("vector", lambda v, par=par, pw=pw:
                                v.tensor_tensor(
                                    yob[:, par * D:(par + 1) * D], pw[:, 0:1024],
                                    xrb[:, par * D:(par + 1) * D], op=OP.add),
                                waits=[wm, xl, yev[par]], inc="sv", drain=False)
                        pwread[par] = ye
                        yev[par] = EM("sync", lambda s, b=b, tc=tc, par=par:
                                      s.dma_start(
                                          out=y_out[b * L + tc * 128:
                                                    b * L + (tc + 1) * 128, :],
                                          in_=yob[:, par * D:(par + 1) * D]),
                                      waits=[ye], inc="dyw%d" % par, amt=16)
                    wo_done = T("st")
                    pvread = pwread[1]
                tok_y_all = [T("dyw0"), T("dyw1")]
                P.flush(block)

        P.fence()
        with contextlib.ExitStack() as stF:
            sbF = lambda *a: stF.enter_context(nc.sbuf_tensor(*a))
            wf1a = sbF("wf1a", [128, 8 * 1024], BF16)
            wf1b = sbF("wf1b", [128, 8 * 1024], BF16)
            wf2s = sbF("wf2s", [128, 8 * 1024], BF16)
            xgt = sbF("xgt", [128, 8 * SPE], BF16)
            hsb = sbF("hsb", [128, 8 * SPE], BF16)
            oacc = sbF("oacc", [128, NSB * D], F32)
            xnb = sbF("xnb", [128, 4 * D], BF16)
            ohb = sbF("ohb", [128, 2 * 2 * CAPT], BF16)
            b1r = sbF("b1r", [32, 128], F32)
            b1c = sbF("b1c", [128, E * 32], F32)
            ps_f = stF.enter_context(nc.psum_tensor("ps_f", [128, 4096], F32))
            wf1 = [wf1a, wf1b]
            HOFF = [0, 512, 1024, 1536, 2048]
            OA, OB = 2560, 3072
            DBASE = [3584, 2560]

            EM("gpsimd", lambda g: g.dma_start(out=b2all[:], in_=b2_in[:]),
               inc="dwa", amt=16)
            tok_b2 = T("dwa")
            prev_b1ev = None
            b1tok = {}
            for e in range(E):
                bpar = e % 2
                l1 = EM("gpsimd", lambda g, e=e: g.dma_start(
                    out=b1r[:],
                    in_=b1_in[e].rearrange("(a b) -> a b", a=32)),
                    waits=[b1tok.get(e - 1)], inc="dwa", amt=16)
                tt = EM("tensor", lambda t: t.transpose(
                    ps_f[0:128, 3584:3584 + 32],
                    b1r[:],
                    identf[0:32, 0:32]),
                    waits=[l1, tok_consts, prev_b1ev], inc="st")
                b1tok[e] = EM("scalar", lambda sc, e=e: sc.copy(
                    b1c[:, e * 32:(e + 1) * 32],
                    ps_f[0:128, 3584:3584 + 32]), waits=[tt], inc="ss")
                prev_b1ev = b1tok[e]

            def w1load(g, e, q, par):
                return g.dma_start(
                    out=wf1[par][:].rearrange("p (c n) -> p c n", c=8),
                    in_=w1_in[e].rearrange("(c p) n -> p c n", p=128)[
                        :, :, q * 1024:(q + 1) * 1024])

            def w2load(g, e, q):
                return g.dma_start(
                    out=wf2s[:].rearrange("p (c n) -> p c n", c=8),
                    in_=w2_in[e, q * 1024:(q + 1) * 1024, :]
                    .rearrange("(c p) n -> p c n", p=128))

            wtok = {}
            for k in range(2):
                EM("gpsimd", lambda g, k=k: w1load(g, k // 4, k % 4, k % 2),
                   inc="dwf%d" % (k % 2), amt=16)
                wtok[k] = T("dwf%d" % (k % 2))
            EM("gpsimd", lambda g: w2load(g, 0, 0), inc="dtb", amt=16)
            w2tok = {0: T("dtb")}

            ohtok = [None, None]
            ohread = [None, None]
            xnread = [None, None]
            dev_tok = [None, None]
            xgt_read = None
            oacc_flush = None
            o_flush_tok = {}
            hread = [None]
            oevh = [None, None]
            w2read = None
            for e in range(E):
                for g2 in range(16):
                    par = g2 % 2
                    xl = EM("sync" if par == 0 else "scalar",
                            lambda h, g2=g2, par=par: h.dma_start(
                                out=xnb[:, par * 2048:(par + 1) * 2048]
                                .rearrange("p (t d) -> p t d", d=D),
                                in_=xn_hbm[g2 * 256:(g2 + 1) * 256, :]
                                .rearrange("(t p) d -> p t d", p=128)),
                            waits=[xnread[par]] + (tok_xn_all
                                                   if (e, g2) == (0, 0)
                                                   else []),
                            inc="dmx%d" % par, amt=16, drain=False)
                    oht = None
                    for t2 in range(2):
                        oht = EM("vector", lambda v, g2=g2, t2=t2, par=par,
                                 e=e: v.tensor_scalar(
                                     ohb[:, par * 2 * CAPT + t2 * CAPT:
                                         par * 2 * CAPT + (t2 + 1) * CAPT],
                                     iota_j[:, 0:CAPT],
                                     jall[:, (g2 * 2 + t2) * E + e:
                                          (g2 * 2 + t2) * E + e + 1],
                                     None, op0=OP.is_equal),
                                 waits=(([tok_route] if (e, g2) == (0, 0)
                                         else []) +
                                        ([ohread[par]] if t2 == 0 and
                                         ohread[par] is not None else [])),
                                 inc="sv" if t2 == 1 else None, drain=False)
                    ohtok[par] = oht
                    for dc in range(8):
                        dpar = dc % 2
                        w = [xl, ohtok[par]]
                        if dev_tok[dpar] is not None:
                            w.append(dev_tok[dpar])
                        if (g2, dc) == (0, 0) and e == 0:
                            w.append(b1tok[E - 1])
                        if (g2, dc) == (0, 1) and oevh[0] is not None:
                            w.append(oevh[0])

                        def dmm(t, g2=g2, dc=dc, par=par, dpar=dpar):
                            r = None
                            for t2 in range(2):
                                r = t.matmul(
                                    ps_f[:, DBASE[dpar] + t2 * CAPT:
                                         DBASE[dpar] + (t2 + 1) * CAPT],
                                    xnb[:, par * 2048 + t2 * D + dc * 128:
                                        par * 2048 + t2 * D + dc * 128 + 128],
                                    ohb[:, par * 2 * CAPT + t2 * CAPT:
                                        par * 2 * CAPT + (t2 + 1) * CAPT],
                                    start=True, stop=True)
                            return r
                        dm = EM("tensor", dmm, waits=w, inc="st")
                        xnread[par] = dm
                        ohread[par] = dm
                        ew = [dm]
                        if (g2, dc) == (0, 0) and xgt_read is not None:
                            ew.append(xgt_read)
                        dev_tok[dpar] = EM(
                            "vector", lambda v, g2=g2, dc=dc, dpar=dpar:
                            v.tensor_copy(
                                xgt[:, dc * SPE + g2 * 2 * CAPT:
                                    dc * SPE + (g2 + 1) * 2 * CAPT],
                                ps_f[:, DBASE[dpar]:DBASE[dpar] + 2 * CAPT]),
                            waits=ew, inc="sv", drain=False)
                disp_done = dev_tok[1]

                for q in range(4):
                    k = 4 * e + q
                    kpar = k % 2
                    hm = None
                    for fb in range(8):
                        for dc in range(8):
                            w = []
                            if dc == 0:
                                w += [t for t in hread if t is not None]
                                if fb == 0:
                                    w += [wtok[k], disp_done]

                            def hmm(t, fb=fb, dc=dc, kpar=kpar):
                                r = None
                                for sb in range(len(HSB)):
                                    r = t.matmul(
                                        ps_f[:, HOFF[sb]:HOFF[sb] + HSB[sb]],
                                        wf1[kpar][:, dc * 1024 + fb * 128:
                                                  dc * 1024 + fb * 128 + 128],
                                        xgt[:, dc * SPE + HOFF[sb]:
                                            dc * SPE + HOFF[sb] + HSB[sb]],
                                        start=(dc == 0), stop=(dc == 7))
                                return r
                            hm = EM("tensor", hmm, waits=w, inc="st")
                        gw = [hm]
                        if fb == 0:
                            gw.append(b1tok[e])
                            if w2read is not None:
                                gw.append(w2read)
                        hread[0] = EM(
                            "scalar", lambda sc, fb=fb, e=e, q=q:
                            sc.activation(
                                hsb[:, fb * SPE:(fb + 1) * SPE],
                                ps_f[:, 0:SPE], AF.Gelu,
                                bias=b1c[:, e * 32 + q * 8 + fb:
                                         e * 32 + q * 8 + fb + 1]),
                            waits=gw, inc="ss", drain=False)
                    xgt_read = hm
                    glast = hread[0]
                    if k + 2 < 4 * E:
                        EM("gpsimd", lambda g, k=k, kpar=kpar: w1load(
                            g, (k + 2) // 4, (k + 2) % 4, kpar),
                            waits=[hm], inc="dwf%d" % kpar, amt=16)
                        wtok[k + 2] = T("dwf%d" % kpar)

                    for s in range(NSB):
                        for hf, base in ((0, OA), (1, OB)):
                            w = []
                            if (s, hf) == (0, 0):
                                w += [w2tok[k], glast]
                            if oevh[hf] is not None:
                                w.append(oevh[hf])

                            def omm(t, s=s, hf=hf, base=base):
                                r = None
                                for fb in range(8):
                                    r = t.matmul(
                                        ps_f[:, base:base + 512],
                                        hsb[:, fb * SPE + s * 128:
                                            fb * SPE + s * 128 + 128],
                                        wf2s[:, fb * 1024 + hf * 512:
                                             fb * 1024 + (hf + 1) * 512],
                                        start=(fb == 0), stop=(fb == 7))
                                return r
                            om = EM("tensor", omm, waits=w, inc="st")
                            ew = [om]
                            if q == 0 and (s, hf) == (0, 0) and \
                                    oacc_flush is not None:
                                ew.append(oacc_flush)
                            dst = (s * D + hf * 512, s * D + (hf + 1) * 512)
                            if q == 0:
                                oevh[hf] = EM(
                                    "vector", lambda v, dst=dst, base=base:
                                    v.tensor_copy(
                                        oacc[:, dst[0]:dst[1]],
                                        ps_f[:, base:base + 512]),
                                    waits=ew, inc="sv", drain=False)
                            else:
                                oevh[hf] = EM(
                                    "vector", lambda v, dst=dst, base=base:
                                    v.tensor_tensor(
                                        oacc[:, dst[0]:dst[1]],
                                        oacc[:, dst[0]:dst[1]],
                                        ps_f[:, base:base + 512],
                                        op=OP.add),
                                    waits=ew, inc="sv", drain=False)
                    w2read = om
                    if k + 1 < 4 * E:
                        EM("gpsimd", lambda g, k=k: w2load(
                            g, (k + 1) // 4, (k + 1) % 4),
                            waits=[om], inc="dtb", amt=16)
                        w2tok[k + 1] = T("dtb")

                oacc_flush = EM("gpsimd", lambda g, e=e: g.dma_start(
                    out=o_dram[e * SPE:(e + 1) * SPE, :]
                    .rearrange("(sb p) d -> p sb d", p=128),
                    in_=oacc[:].rearrange("p (sb d) -> p sb d", d=D)),
                    waits=[oevh[0], oevh[1]], inc="dsc", amt=16)
                o_flush_tok[e] = oacc_flush
            P.flush(block)

        P.fence()
        with contextlib.ExitStack() as stC:
            sbC = lambda *a: stC.enter_context(nc.sbuf_tensor(*a))
            obuf = sbC("obuf", [CAPT, 2 * E * D], BF16)
            ytb = sbC("ytb", [128, 2 * D], F32)
            ysb = sbC("ysb", [128, 2 * D], F32)
            ohc = sbC("ohc", [128, CAPT], BF16)
            ohgc = sbC("ohgc", [128, CAPT], F32)
            ohgT = sbC("ohgT", [CAPT, 16 * 128], BF16)
            gTt = sbC("gTt", [8, 2 * 128], BF16)
            ps_y = stC.enter_context(nc.psum_tensor("ps_y", [128, 2048], F32))
            ps_tr = stC.enter_context(nc.psum_tensor("ps_tr", [128, 512], F32))

            ytread = [None, None]
            obread = [None, None]
            ohg_read = [None, None]
            trev = [None, None]
            prev_tr = None
            prev_gev = None
            ywr_tok = [None, None]
            pend_fin = [None]

            def finalize_tile(i):
                par = i % 2
                mml, ytl = fin_state[par]
                tts = None
                for f in range(2):
                    tw = []
                    if f == 0:
                        tw = [mml, ytl]
                        if ywr_tok[par] is not None:
                            tw.append(ywr_tok[par])
                    tts = EM("vector", lambda v, par=par, f=f: v.tensor_tensor(
                        ysb[:, par * D + f * 512:par * D + (f + 1) * 512],
                        ps_y[:, par * 1024 + f * 512:
                             par * 1024 + (f + 1) * 512],
                        ytb[:, par * D + f * 512:par * D + (f + 1) * 512],
                        op=OP.add), waits=tw, inc="sv" if f == 1 else None,
                        drain=False)
                ytread[par] = tts
                ywr_tok[par] = EM("sync", lambda s, i=i, par=par: s.dma_start(
                    out=y_out[i * 128:(i + 1) * 128, :],
                    in_=ysb[:, par * D:(par + 1) * D]),
                    waits=[tts], inc="dyw%d" % par, amt=16)

            fin_state = [None, None]
            for i in range(NT):
                par = i % 2
                ytl = EM("sync", lambda s, i=i, par=par: s.dma_start(
                    out=ytb[:, par * D:(par + 1) * D],
                    in_=y_out[i * 128:(i + 1) * 128, :]),
                    waits=(tok_y_all if i < 2 else []) +
                    ([ytread[par]] if ytread[par] is not None else []),
                    inc="dmx%d" % par, amt=16)
                obl = []
                for e2 in range(E):
                    obl.append(EM("scalar", lambda sc, i=i, par=par, e2=e2:
                                  sc.dma_start(
                                      out=obuf[:, par * 8192 + e2 * D:
                                               par * 8192 + (e2 + 1) * D],
                                      in_=o_dram[e2 * SPE + i * CAPT:
                                                 e2 * SPE + (i + 1) * CAPT, :]),
                                  waits=([o_flush_tok[e2]] if i < 2 else []) +
                                  ([obread[par]] if e2 == 0 and
                                   obread[par] is not None else []),
                                  inc="dro" if par == 0 else "dcc", amt=16,
                                  drain=False))
                gtr = EM("tensor", lambda t, i=i: t.transpose(
                    ps_tr[0:8, 256:384], gall[:, i * E:(i + 1) * E],
                    identf[:]),
                    waits=[prev_gev] + ([ohg_read[par]]
                                        if ohg_read[par] is not None else []),
                    inc="st")
                gev = EM("vector", lambda v, par=par: v.tensor_copy(
                    gTt[:, par * 128:(par + 1) * 128], ps_tr[0:8, 256:384]),
                    waits=[gtr], inc="sv", drain=False)
                prev_gev = gev

                ev8 = []
                for e2 in range(E):
                    pe = e2 % 2
                    oh2 = EM("vector", lambda v, i=i, e2=e2: v.tensor_scalar(
                        ohgc[:], iota_j[:, 0:CAPT],
                        jall[:, i * E + e2:i * E + e2 + 1],
                        gall[:, i * E + e2:i * E + e2 + 1],
                        op0=OP.is_equal, op1=OP.mult),
                        waits=[prev_tr] if prev_tr is not None else [],
                        inc="sv", drain=False)
                    tr = EM("tensor", lambda t, pe=pe: t.transpose(
                        ps_tr[0:CAPT, pe * 128:(pe + 1) * 128], ohgc[:],
                        identf[:]), waits=[oh2] +
                        ([trev[pe]] if trev[pe] is not None else []),
                        inc="st")
                    prev_tr = tr
                    slot = (i % 2) * 8 + e2
                    ew = [tr]
                    if e2 == 0 and ohg_read[par] is not None:
                        ew.append(ohg_read[par])
                    trev[pe] = EM("vector", lambda v, pe=pe, slot=slot:
                                  v.tensor_copy(
                                      ohgT[:, slot * 128:(slot + 1) * 128],
                                      ps_tr[0:CAPT, pe * 128:(pe + 1) * 128]),
                                  waits=ew, inc="sv", drain=False)
                    ev8.append(trev[pe])

                if i > 0:
                    finalize_tile(i - 1)

                mml = None
                for e2 in range(E):
                    slot = (i % 2) * 8 + e2
                    w = [ev8[e2], obl[e2]]
                    if e2 == 0:
                        w += [gev, tok_b2]
                        if ytread[par] is not None:
                            w.append(ytread[par])

                    def cmb(t, e2=e2, par=par, slot=slot):
                        r = None
                        if e2 == 0:
                            for f in range(2):
                                r = t.matmul(
                                    ps_y[:, par * 1024 + f * 512:
                                         par * 1024 + (f + 1) * 512],
                                    gTt[:, par * 128:(par + 1) * 128],
                                    b2all[:, f * 512:(f + 1) * 512],
                                    start=True, stop=False)
                        for f in range(2):
                            r = t.matmul(
                                ps_y[:, par * 1024 + f * 512:
                                     par * 1024 + (f + 1) * 512],
                                ohgT[:, slot * 128:(slot + 1) * 128],
                                obuf[:, par * 8192 + e2 * D + f * 512:
                                     par * 8192 + e2 * D + (f + 1) * 512],
                                start=False, stop=(e2 == E - 1))
                        return r
                    mml = EM("tensor", cmb, waits=w, inc="st")
                    obread[par] = mml
                ohg_read[par] = mml
                fin_state[par] = (mml, ytl)
            finalize_tile(NT - 1)

            EM("gpsimd", lambda g: g.nop(),
               waits=[T("dyw0"), T("dyw1"), T("dsn0"), T("dsn1"), T("dsc")])
            P.flush(block)

    nc.finalize()
    return nc


_NC_CACHE = [None]


def kernel(**inputs):
    x = np.ascontiguousarray(np.asarray(inputs["L0_attn_x"], dtype=np.float32))
    wq = np.ascontiguousarray(np.asarray(inputs["L0_attn_Wq"], dtype=np.float32))
    wk = np.ascontiguousarray(np.asarray(inputs["L0_attn_Wk"], dtype=np.float32))
    wv = np.ascontiguousarray(np.asarray(inputs["L0_attn_Wv"], dtype=np.float32))
    wo = np.ascontiguousarray(np.asarray(inputs["L0_attn_Wo"], dtype=np.float32))
    wr = np.ascontiguousarray(np.asarray(inputs["L0_ffn_Wr"], dtype=np.float32))
    br = np.ascontiguousarray(np.asarray(inputs["L0_ffn_br"], dtype=np.float32))
    w1 = np.ascontiguousarray(np.asarray(inputs["L0_ffn_W1"], dtype=np.float32))
    b1 = np.ascontiguousarray(np.asarray(inputs["L0_ffn_b1"], dtype=np.float32))
    w2 = np.ascontiguousarray(np.asarray(inputs["L0_ffn_W2"], dtype=np.float32))
    b2 = np.ascontiguousarray(np.asarray(inputs["L0_ffn_b2"], dtype=np.float32))

    cosq, sinq, _, _ = _host_tables()
    xf = x.reshape(B * L, D)

    if _NC_CACHE[0] is None:
        _NC_CACHE[0] = build_nc()
    nc = _NC_CACHE[0]

    in_maps = []
    for c in range(NCORES):
        pm = np.zeros((E, 1), np.float32)
        pm[:c, 0] = 1.0
        in_maps.append({
            "x": xf[c * TPC:(c + 1) * TPC],
            "wq": wq, "wk": wk, "wv": wv, "wo": wo,
            "wr": wr, "br": br.reshape(1, E),
            "w1": w1, "b1": b1, "w2": w2, "b2": b2,
            "cosq": cosq, "sinq": sinq,
            "prevmask": pm,
        })
    res = run_bass_kernel_spmd(nc, in_maps, core_ids=list(range(NCORES)))
    out = np.concatenate([res.results[c]["y"] for c in range(NCORES)], axis=0)
    return out.reshape(B, L, D)
